# version 25
# baseline (speedup 1.0000x reference)
import os
import numpy as np
import ml_dtypes
from contextlib import ExitStack

import concourse.bass as bass
import concourse.mybir as mybir
import concourse.alu_op_type as aot
from concourse.bass_utils import run_bass_kernel_spmd

F32 = mybir.dt.float32
BF16 = mybir.dt.bfloat16
I32 = mybir.dt.int32
AF = mybir.ActivationFunctionType
ALU = aot.AluOpType
AX = mybir.AxisListType.X

D = 2048
L = 2048
NCORE = 8
H = 16
DK = 64
DV = 128
HID = 5632
NKT = D // 128
TG = 512
NTG = L // TG
CH = 128
NCH = L // CH
SCALE = float(128.0 ** -0.5)
EPS = 1e-5
NHS = HID // 128
DEBUG = False

ENGS = ("sync", "gpsimd", "scalar", "vector", "tensor")
NSLOT = 16


class Op:
    __slots__ = ("eng", "fn", "reads", "writes", "dma", "rows", "psum",
                 "deps", "need_inc", "sem", "val", "waits", "barrier", "unsafe")

    def __init__(self, eng, fn, reads, writes, dma, rows, psum, barrier=False):
        self.unsafe = False
        self.eng = eng
        self.fn = fn
        self.reads = tuple(reads)
        self.writes = tuple(writes)
        self.dma = dma
        self.rows = rows
        self.psum = psum
        self.deps = []
        self.need_inc = dma
        self.sem = None
        self.val = 0
        self.waits = []
        self.barrier = barrier


class Prog:
    def __init__(self, nc, stack):
        self.nc = nc
        self.ops = []
        self.eng_sem = {e: stack.enter_context(nc.semaphore(f"S_{e}")) for e in ENGS}
        self.slot_sem = {}
        for q in ("sync", "gpsimd", "scalar"):
            for i in range(NSLOT):
                self.slot_sem[(q, i)] = stack.enter_context(nc.semaphore(f"D_{q}_{i}"))
        self.slot_rr = {"sync": 0, "gpsimd": 0, "scalar": 0}

    def op(self, eng, fn, reads=(), writes=(), rows=None, psum=None, unsafe=False):
        o = Op(eng, fn, reads, writes, False, rows, psum)
        o.unsafe = unsafe
        self.ops.append(o)

    def dma(self, eng, fn, reads=(), writes=()):
        o = Op(eng, fn, reads, writes, True, None, None)
        s = self.slot_rr[eng]
        self.slot_rr[eng] = (s + 1) % NSLOT
        o.writes = tuple(writes) + (f"__slot:{eng}:{s}",)
        o.sem = (eng, s)
        self.ops.append(o)

    def barrier(self):
        self.ops.append(Op(None, None, (), (), False, None, None, barrier=True))

    def resolve(self):
        region_w = {}
        region_r = {}
        pe_active = []
        last_pe = None
        for i, o in enumerate(self.ops):
            if o.barrier:
                region_r = {k: [] for k in region_r}
                pe_active = []
                continue
            deps = set()
            for r in o.reads:
                if r in region_w:
                    deps.add(region_w[r])
            for w in o.writes:
                if w in region_w:
                    deps.add(region_w[w])
                for rd in region_r.get(w, ()):
                    deps.add(rd)
            if o.eng == "tensor" and o.rows is not None:
                conflict = False
                for (rws, ps, oi) in pe_active:
                    if ps is not None and ps == o.psum:
                        a0, a1 = rws[0], rws[0] + rws[1]
                        b0, b1 = o.rows[0], o.rows[0] + o.rows[1]
                        if a1 <= b0 or b1 <= a0:
                            conflict = True
                if conflict and last_pe is not None:
                    deps.add(last_pe)
                    pe_active = []
                if o.rows == (0, 128):
                    pe_active = [(o.rows, o.psum, i)]
                else:
                    pe_active.append((o.rows, o.psum, i))
                last_pe = i
            kept = []
            for d in deps:
                do = self.ops[d]
                if ((not do.dma) and (not o.dma) and do.eng == o.eng
                        and o.eng == "tensor" and not do.unsafe):
                    continue
                kept.append(d)
            o.deps = kept
            for d in kept:
                self.ops[d].need_inc = True
            for w in o.writes:
                region_w[w] = i
                region_r[w] = []
            for r in o.reads:
                region_r.setdefault(r, []).append(i)
        eng_cnt = {e: 0 for e in ENGS}
        slot_cnt = {k: 0 for k in self.slot_sem}
        for o in self.ops:
            if o.barrier:
                continue
            if o.dma:
                slot_cnt[o.sem] += 16
                o.val = slot_cnt[o.sem]
            elif o.need_inc:
                eng_cnt[o.eng] += 1
                o.sem = o.eng
                o.val = eng_cnt[o.eng]
        maxseen = {e: {} for e in ENGS}
        self.barrier_waits = []
        eng_cnt2 = {e: 0 for e in ENGS}
        slot_cnt2 = {k: 0 for k in self.slot_sem}
        lastop = {e: None for e in ENGS}
        for o in self.ops:
            if o.barrier:
                for e in ENGS:
                    lo = lastop[e]
                    if lo is not None and (not lo.dma) and not lo.need_inc:
                        lo.need_inc = True
            elif not o.dma:
                lastop[o.eng] = o
        eng_cnt = {e: 0 for e in ENGS}
        slot_cnt = {k: 0 for k in self.slot_sem}
        for o in self.ops:
            if o.barrier:
                continue
            if o.dma:
                slot_cnt[o.sem] += 16
                o.val = slot_cnt[o.sem]
            elif o.need_inc:
                eng_cnt[o.eng] += 1
                o.sem = o.eng
                o.val = eng_cnt[o.eng]
            else:
                o.sem = None
        run_eng = {e: 0 for e in ENGS}
        run_slot = {k: 0 for k in self.slot_sem}
        for o in self.ops:
            if o.barrier:
                wl = [(("E", e), run_eng[e]) for e in ENGS if run_eng[e] > 0]
                wl += [(s, run_slot[s]) for s in self.slot_sem if run_slot[s] > 0]
                o.waits = wl
                for e in ENGS:
                    for (sk, v) in wl:
                        ms = maxseen[e]
                        if ms.get(sk, 0) < v:
                            ms[sk] = v
                continue
            wl = []
            for d in o.deps:
                do = self.ops[d]
                sk = ("E", do.sem) if not do.dma else do.sem
                if do.dma:
                    sk = do.sem
                else:
                    sk = ("E", do.eng)
                wl.append((sk, do.val))
            ms = maxseen[o.eng]
            wl2 = []
            for (sk, v) in sorted(set(wl)):
                if ms.get(sk, 0) >= v:
                    continue
                best = max((vv for (kk, vv) in wl if kk == sk), default=v)
                if ms.get(sk, 0) < best:
                    ms[sk] = best
                    wl2.append((sk, best))
            o.waits = wl2
            if o.dma:
                run_slot[o.sem] = o.val
            elif o.need_inc:
                run_eng[o.eng] = o.val

    def emit(self, block):
        def semof(sk):
            if isinstance(sk, tuple) and sk[0] == "E":
                return self.eng_sem[sk[1]]
            return self.slot_sem[sk]

        def mkbody(ename):
            def body(eng):
                for o in self.ops:
                    if o.barrier:
                        for (sk, v) in o.waits:
                            eng.wait_ge(semof(sk), v)
                        continue
                    if o.eng != ename:
                        continue
                    for (sk, v) in o.waits:
                        eng.wait_ge(semof(sk), v)
                    inst = o.fn(eng)
                    if o.dma:
                        inst.then_inc(self.slot_sem[o.sem], 16)
                    elif o.need_inc:
                        inst.then_inc(self.eng_sem[o.eng], 1)
            return body

        block.sync(mkbody("sync"))
        block.gpsimd(mkbody("gpsimd"))
        block.scalar(mkbody("scalar"))
        block.vector(mkbody("vector"))
        block.tensor(mkbody("tensor"))


def build_nc(debug=DEBUG):
    nc = bass.Bass("TRN2", target_bir_lowering=False)
    dt = nc.dram_tensor
    x_h = dt("x", [L, D], F32, kind="ExternalInput").ap()
    wqT_h = dt("wqT", [D, 1024], F32, kind="ExternalInput").ap()
    wkT_h = dt("wkT", [D, 1024], F32, kind="ExternalInput").ap()
    wvT_h = dt("wvT", [D, D], F32, kind="ExternalInput").ap()
    wgaT_h = dt("wgaT", [D, D], F32, kind="ExternalInput").ap()
    woT_h = dt("woT", [D, D], F32, kind="ExternalInput").ap()
    wkg1T_h = dt("wkg1T", [D, 16], F32, kind="ExternalInput").ap()
    wkg2a_h = dt("wkg2a", [17, 1024], F32, kind="ExternalInput").ap()
    wgateT_h = dt("wgateT", [D, HID], I32, kind="ExternalInput").ap()
    wupT_h = dt("wupT", [D, HID], I32, kind="ExternalInput").ap()
    wdownT_h = dt("wdownT", [HID, D], I32, kind="ExternalInput").ap()
    sgate_h = dt("sgate_t", [128, NHS], F32, kind="ExternalInput").ap()
    sup_h = dt("sup_t", [128, NHS], F32, kind="ExternalInput").ap()
    sdown_h = dt("sdown_t", [128, 16], F32, kind="ExternalInput").ap()
    bg_h = dt("bg_t", [128, 16], F32, kind="ExternalInput").ap()
    ident_h = dt("ident", [128, 128], BF16, kind="ExternalInput").ap()
    uneg_h = dt("uneg", [128, 128], BF16, kind="ExternalInput").ap()
    mask4_h = dt("mask4", [128, 256], BF16, kind="ExternalInput").ap()
    onescol_h = dt("onescol", [128, 1], BF16, kind="ExternalInput").ap()
    onesrow_h = dt("onesrow", [1, TG], F32, kind="ExternalInput").ap()
    neg128_h = dt("neg128", [1, 128], F32, kind="ExternalInput").ap()
    sdrep_h = dt("sdrep_in", [128, D], F32, kind="ExternalInput").ap()
    out_h = dt("out", [L, D], F32, kind="ExternalOutput").ap()
    skind = "ExternalOutput" if debug else "Internal"
    xn_s = dt("xn_s", [L, D], BF16, kind=skind).ap()
    wdbf_s = dt("wdbf_s", [HID, D], BF16, kind="Internal").ap()
    wobf_s = dt("wobf_s", [D, D], BF16, kind="Internal").ap()
    wqbf_s = dt("wqbf_s", [D, 1024], BF16, kind="Internal").ap()
    wkbf_s = dt("wkbf_s", [D, 1024], BF16, kind="Internal").ap()
    wvbf_s = dt("wvbf_s", [D, D], BF16, kind="Internal").ap()
    wkg1bf_s = dt("wkg1bf_s", [D, 16], BF16, kind="Internal").ap()
    wgabf_s = dt("wgabf_s", [D, D], BF16, kind="Internal").ap()
    o_s = dt("o_s", [L, D], BF16, kind=skind).ap()
    hid_s = dt("hid_s", [HID, L], BF16, kind=skind).ap()
    sg_s = dt("sg_s", [D, L], BF16, kind=skind).ap()
    g_s = dt("g_s", [L, D], BF16, kind=skind).ap()
    s_s = dt("s_s", [L, D], BF16, kind=skind).ap()

    st = ExitStack()
    with st:
        P = Prog(nc, st)
        sb = lambda name, shape, dtyp: st.enter_context(nc.sbuf_tensor(name, shape, dtyp))
        c_ident = sb("c_ident", [128, 128], BF16)
        c_uneg = sb("c_uneg", [128, 128], BF16)
        c_mask4 = sb("c_mask4", [128, 256], BF16)
        c_ones = sb("c_ones", [128, 1], BF16)
        c_neg128 = sb("c_neg128", [1, 128], F32)
        sb_sgate = sb("sb_sgate", [128, NHS], F32)
        sb_sup = sb("sb_sup", [128, NHS], F32)
        sb_sdown = sb("sb_sdown", [128, 16], F32)
        sb_bg = sb("sb_bg", [128, 16], F32)

        PS = [st.enter_context(nc.psum_tensor(f"PS{i}", [128, 512], F32)) for i in range(7)]
        PT = st.enter_context(nc.psum_tensor("PT", [128, 1024], BF16))

        for (t, h, key) in ((c_ident, ident_h, "c_ident"), (c_uneg, uneg_h, "c_uneg"),
                            (c_mask4, mask4_h, "c_mask4"), (c_ones, onescol_h, "c_ones"),
                            (c_neg128, neg128_h, "c_neg128"), (sb_sgate, sgate_h, "sb_sgate"),
                            (sb_sup, sup_h, "sb_sup"), (sb_sdown, sdown_h, "sb_sdown"),
                            (sb_bg, bg_h, "sb_bg")):
            P.dma("sync", lambda e, t=t, h=h: e.dma_start(out=t[:, :], in_=h[:, :]), writes=[key])


        for (src, dst, key, nch) in ((wkg1T_h, wkg1bf_s, "wkg1bf", 1), (wqT_h, wqbf_s, "wqbf", 2),
                                     (wkT_h, wkbf_s, "wkbf", 2), (wvT_h, wvbf_s, "wvbf", 4)):
            rows = src.shape[0] // nch
            for c in range(nch):
                P.dma("gpsimd", lambda e, src=src, dst=dst, c=c, rows=rows: e.dma_start(
                    out=dst[c * rows:(c + 1) * rows, :], in_=src[c * rows:(c + 1) * rows, :]),
                    writes=[f"{key}:{c}"])

        sx = ExitStack()
        xnT = [sx.enter_context(nc.sbuf_tensor(f"xnT{k}", [128, L], BF16)) for k in range(NKT)]
        with ExitStack() as s0:
            sb0 = lambda n, shp, d: s0.enter_context(nc.sbuf_tensor(n, shp, d))
            xb = [sb0(f"xb{i}", [128, D], F32) for i in range(8)]
            dump = sb0("dump", [128, D], BF16)
            dump2 = sb0("dump2", [128, D], BF16)
            xnb = [sb0(f"xnb{i}", [128, D], BF16) for i in range(2)]
            sb4 = [sb0(f"sb4_{i}", [128, 36], F32) for i in range(2)]
            for b4 in range(4):
                bp = b4 % 2
                stk = f"sb4_{bp}"
                for i in range(4):
                    tt = 4 * b4 + i
                    ii = tt % 8
                    P.dma("sync", lambda e, tt=tt, ii=ii: e.dma_start(out=xb[ii][:, :], in_=x_h[tt * 128:(tt + 1) * 128, :]),
                          writes=[f"xb{ii}"])
                for i in range(4):
                    ii = (4 * b4 + i) % 8
                    P.op("vector", lambda e, ii=ii, i=i, bp=bp: e.tensor_reduce(sb4[bp][:, i:i + 1], xb[ii][:, :], AX,
                                                                                ALU.add, negate=True),
                         reads=[f"xb{ii}"], writes=[stk], unsafe=True)
                    if i < 2:
                        P.op("scalar", lambda e, ii=ii, i=i, bp=bp: e.activation(dump2[:, :], xb[ii][:, :], AF.Square,
                                                                                 bias=0.0, scale=1.0,
                                                                                 accum_out=sb4[bp][:, 4 + i:5 + i]),
                             reads=[f"xb{ii}"], writes=["dump2", stk], unsafe=True)
                    else:
                        P.op("vector", lambda e, ii=ii, i=i, bp=bp: e.scalar_tensor_tensor(dump[:, :], xb[ii][:, :], 1.0,
                                                                                           xb[ii][:, :], ALU.mult, ALU.mult,
                                                                                           accum_out=sb4[bp][:, 4 + i:5 + i]),
                             reads=[f"xb{ii}"], writes=["dump", stk], unsafe=True)
                P.op("vector", lambda e, bp=bp: e.tensor_scalar(sb4[bp][:, 8:12], sb4[bp][:, 0:4], 1.0 / D, 0.0,
                                                                ALU.mult, ALU.add), reads=[stk], writes=[stk], unsafe=True)
                P.op("vector", lambda e, bp=bp: e.tensor_scalar(sb4[bp][:, 12:16], sb4[bp][:, 4:8], 1.0 / D, 0.0,
                                                                ALU.mult, ALU.add), reads=[stk], writes=[stk], unsafe=True)
                P.op("vector", lambda e, bp=bp: e.tensor_tensor(sb4[bp][:, 16:20], sb4[bp][:, 8:12], sb4[bp][:, 8:12],
                                                                ALU.mult), reads=[stk], writes=[stk], unsafe=True)
                P.op("vector", lambda e, bp=bp: e.scalar_tensor_tensor(sb4[bp][:, 20:24], sb4[bp][:, 12:16], EPS,
                                                                       sb4[bp][:, 16:20], ALU.add, ALU.subtract),
                     reads=[stk], writes=[stk], unsafe=True)
                P.op("scalar", lambda e, bp=bp: e.activation(sb4[bp][:, 24:28], sb4[bp][:, 20:24], AF.Ln,
                                                             bias=0.0, scale=1.0), reads=[stk], writes=[stk])
                P.op("scalar", lambda e, bp=bp: e.activation(sb4[bp][:, 28:32], sb4[bp][:, 24:28], AF.Exp,
                                                             bias=0.0, scale=-0.5), reads=[stk], writes=[stk])
                P.op("vector", lambda e, bp=bp: e.tensor_tensor(sb4[bp][:, 32:36], sb4[bp][:, 8:12], sb4[bp][:, 28:32],
                                                                ALU.mult), reads=[stk], writes=[stk], unsafe=True)
                for i in range(4):
                    tt = 4 * b4 + i
                    ii = tt % 8
                    xp = i % 2
                    P.op("scalar", lambda e, ii=ii, i=i, bp=bp, xp=xp: e.activation(
                        xnb[xp][:, :], xb[ii][:, :], AF.Identity,
                        bias=sb4[bp][:, 32 + i:33 + i], scale=sb4[bp][:, 28 + i:29 + i]),
                        reads=[f"xb{ii}", stk], writes=[f"xnb{xp}"])
                    if debug:
                        P.dma("gpsimd", lambda e, tt=tt, xp=xp: e.dma_start(
                            out=xn_s[tt * 128:(tt + 1) * 128, :], in_=xnb[xp][:, :]),
                            reads=[f"xnb{xp}"], writes=[f"xn_s:{tt}"])
                    for half in range(2):
                        for j in range(8):
                            kk = 8 * half + j
                            P.op("tensor", lambda e, kk=kk, j=j, xp=xp: e.transpose(
                                PT[:, j * 128:(j + 1) * 128], xnb[xp][:, kk * 128:(kk + 1) * 128], c_ident[:, :]),
                                reads=[f"xnb{xp}", "c_ident"], writes=["PT"], rows=(0, 128), psum="PT")
                        for j in range(8):
                            kk = 8 * half + j
                            P.op("vector", lambda e, kk=kk, j=j, tt=tt: e.tensor_copy(
                                xnT[kk][:, tt * 128:(tt + 1) * 128], PT[:, j * 128:(j + 1) * 128]),
                                reads=["PT"], writes=[f"xnT:{kk}"])
        P.barrier()

        with ExitStack() as sa:
            sba = lambda n, shp, d: sa.enter_context(nc.sbuf_tensor(n, shp, d))
            wq_sb = [sba(f"wq{k}", [128, 512], BF16) for k in range(NKT)]
            wk_sb = [sba(f"wk{k}", [128, 512], BF16) for k in range(NKT)]
            wv_sb = [sba(f"wv{k}", [128, 1024], BF16) for k in range(NKT)]
            wkg1_sb = [sba(f"wkg1{k}", [128, 16], BF16) for k in range(NKT)]
            wkg2_sb = sba("wkg2", [17, 512], F32)
            kg1_sb = sba("kg1", [17, 512], F32)
            qT_sb = [[sba(f"qT{c}_{f}", [128, 512], BF16) for f in range(4)] for c in range(2)]
            kT_sb = [[sba(f"kT{c}_{f}", [128, 512], BF16) for f in range(4)] for c in range(2)]
            gkneg = [[sba(f"gkn{c}_{t}", [128, 512], BF16) for t in range(4)] for c in range(2)]
            etmp = [sba(f"etmp{j}", [128, 512], BF16) for j in range(2)]
            v_sb = [[sba(f"v{c}_{t}", [128, 1024], BF16) for t in range(4)] for c in range(2)]
            expA = [sba(f"expA{c}", [128, 512], F32) for c in range(2)]
            expNA = [sba(f"expNA{c}", [128, 512], BF16) for c in range(2)]
            qe_sb = [[sba(f"qe{c}_{f}", [128, 128], BF16) for f in range(4)] for c in range(2)]
            ke_sb = [[sba(f"ke{c}_{f}", [128, 128], BF16) for f in range(4)] for c in range(2)]
            khn_sb = [sba(f"khn{c}", [128, 512], BF16) for c in range(2)]
            scm_sb = [[sba(f"scm{c}_{j}", [128, 256], BF16) for j in range(2)] for c in range(2)]
            S_f32 = [sba(f"Sf{f}", [128, 128], F32) for f in range(4)]
            S_bf = [sba(f"Sb{f}", [128, 128], BF16) for f in range(4)]
            o_raw = [[sba(f"oraw{c}_{j}", [128, 128], F32) for j in range(4)] for c in range(2)]
            hsum = [sba(f"hsum{c}", [128, 8], F32) for c in range(2)]
            hsq = [sba(f"hsq{c}", [128, 8], F32) for c in range(2)]
            hwork = [sba(f"hwork{c}", [128, 16], F32) for c in range(2)]
            sdump = sba("sdump", [128, 128], BF16)
            o_sb = [sba(f"osb{j}", [128, 1024], BF16) for j in range(2)]
            stmp = [sba(f"stmp{c}", [128, 128], F32) for c in range(2)]

            def load_pass_weights(pp):
                for k in range(NKT):
                    P.dma("scalar", lambda e, k=k, pp=pp: e.dma_start(
                        out=wkg1_sb[k][:, :], in_=wkg1bf_s[k * 128:(k + 1) * 128, 0:16]),
                        reads=["wkg1bf:0"], writes=[f"wkg1:{k}"])
                P.dma("sync", lambda e, pp=pp: e.dma_start(out=wkg2_sb[:, :], in_=wkg2a_h[:, pp * 512:pp * 512 + 512]),
                      writes=["wkg2"])
                if pp == 0:
                    P.dma("sync", lambda e: e.dma_start(out=kg1_sb[16:17, :], in_=onesrow_h[:, :]), writes=["kg1row"])
                for k in range(NKT):
                    P.dma("sync", lambda e, k=k, pp=pp: e.dma_start(
                        out=wq_sb[k][:, :], in_=wqbf_s[k * 128:(k + 1) * 128, pp * 512:pp * 512 + 512]),
                        reads=[f"wqbf:{k // 8}"], writes=[f"wq:{k}"])
                    P.dma("scalar", lambda e, k=k, pp=pp: e.dma_start(
                        out=wk_sb[k][:, :], in_=wkbf_s[k * 128:(k + 1) * 128, pp * 512:pp * 512 + 512]),
                        reads=[f"wkbf:{k // 8}"], writes=[f"wk:{k}"])
                for k in range(NKT):
                    q = "sync" if k % 2 == 0 else "scalar"
                    P.dma(q, lambda e, k=k, pp=pp: e.dma_start(
                        out=wv_sb[k][:, :], in_=wvbf_s[k * 128:(k + 1) * 128, pp * 1024:(pp + 1) * 1024]),
                        reads=[f"wvbf:{k // 4}"], writes=[f"wv:{k}"])

            def s_init():
                for f in range(4):
                    P.op("vector", lambda e, f=f: e.memset(S_f32[f][:, :], 0.0), writes=[f"Sf:{f}"])
                    P.op("vector", lambda e, f=f: e.memset(S_bf[f][:, :], 0.0), writes=[f"Sb:{f}"])

            def make_proj_groups(gseg):
                scp = gseg % 2
                T0 = (gseg % 4) * 512
                groups = []
                pa = [0]

                def q_group(f):
                    def emit():
                        ps, psk = PS[pa[0] % 2], f"PS{pa[0] % 2}"
                        pa[0] += 1
                        for k in range(NKT):
                            P.op("tensor", lambda e, k=k, ps=ps, f=f, T0=T0: e.matmul(
                                ps[:, 0:512], wq_sb[k][:, f * 128:(f + 1) * 128], xnT[k][:, T0:T0 + 512],
                                start=(k == 0), stop=(k == NKT - 1)),
                                reads=[f"wq:{k}", f"xnT:{k}"], writes=[psk], rows=(0, 128), psum=psk)
                        P.op("scalar", lambda e, ps=ps, f=f, scp=scp: e.copy(qT_sb[scp][f][:, :], ps[:, 0:512]),
                             reads=[psk], writes=[f"qT{scp}:{f}"])
                    return emit

                def k_group(f):
                    def emit():
                        ps, psk = PS[pa[0] % 2], f"PS{pa[0] % 2}"
                        pa[0] += 1
                        for k in range(NKT):
                            P.op("tensor", lambda e, k=k, ps=ps, f=f, T0=T0: e.matmul(
                                ps[:, 0:512], wk_sb[k][:, f * 128:(f + 1) * 128], xnT[k][:, T0:T0 + 512],
                                start=(k == 0), stop=(k == NKT - 1)),
                                reads=[f"wk:{k}", f"xnT:{k}"], writes=[psk], rows=(0, 128), psum=psk)
                        P.op("scalar", lambda e, ps=ps, f=f, scp=scp: e.activation(
                            kT_sb[scp][f][:, :], ps[:, 0:512], AF.Copy, bias=0.0, scale=SCALE),
                            reads=[psk], writes=[f"kT{scp}:{f}"])
                    return emit

                def kg1_group():
                    def emit():
                        ps, psk = PS[pa[0] % 2], f"PS{pa[0] % 2}"
                        pa[0] += 1
                        for k in range(NKT):
                            P.op("tensor", lambda e, k=k, ps=ps, T0=T0: e.matmul(
                                ps[0:16, 0:512], wkg1_sb[k][:, :], xnT[k][:, T0:T0 + 512],
                                start=(k == 0), stop=(k == NKT - 1)),
                                reads=[f"wkg1:{k}", f"xnT:{k}"], writes=[psk], rows=(0, 128), psum=psk)
                        P.op("vector", lambda e, ps=ps: e.tensor_copy(kg1_sb[0:16, :], ps[0:16, 0:512]),
                             reads=[psk], writes=["kg1"])
                    return emit

                def kgnat_group(ts):
                    def emit():
                        ps, psk = PS[pa[0] % 2], f"PS{pa[0] % 2}"
                        pa[0] += 1
                        P.op("tensor", lambda e, ts=ts, ps=ps: e.matmul(
                            ps[:, 0:512], kg1_sb[:, ts * 128:(ts + 1) * 128], wkg2_sb[:, :],
                            start=True, stop=True),
                            reads=["kg1", "kg1row", "wkg2"], writes=[psk], rows=(0, 17), psum=psk)
                        je = ts % 2
                        P.op("scalar", lambda e, ps=ps, je=je: e.activation(etmp[je][:, :], ps[:, 0:512],
                                                                            AF.Exp, bias=0.0, scale=-1.0),
                             reads=[psk], writes=[f"etmp{je}"])
                        P.op("scalar", lambda e, ts=ts, je=je, scp=scp: e.activation(
                            gkneg[scp][ts][:, :], etmp[je][:, :], AF.Ln, bias=1.0, scale=1.0),
                            reads=[f"etmp{je}"], writes=[f"gkn{scp}:{ts}"])
                    return emit

                def v_group(ts, vf):
                    def emit():
                        ps, psk = PS[pa[0] % 2], f"PS{pa[0] % 2}"
                        pa[0] += 1
                        for k in range(NKT):
                            P.op("tensor", lambda e, k=k, ts=ts, vf=vf, ps=ps, T0=T0: e.matmul(
                                ps[:, 0:512], xnT[k][:, T0 + ts * 128:T0 + ts * 128 + 128],
                                wv_sb[k][:, vf * 512:(vf + 1) * 512],
                                start=(k == 0), stop=(k == NKT - 1)),
                                reads=[f"wv:{k}", f"xnT:{k}"], writes=[psk], rows=(0, 128), psum=psk)
                        P.op("scalar", lambda e, ts=ts, vf=vf, ps=ps, scp=scp: e.copy(
                            v_sb[scp][ts][:, vf * 512:(vf + 1) * 512], ps[:, 0:512]),
                            reads=[psk], writes=[f"v{scp}:{ts}"])
                    return emit

                groups.append(kg1_group())
                for ts in range(4):
                    groups.append(kgnat_group(ts))
                for f in range(4):
                    groups.append(q_group(f))
                    groups.append(k_group(f))
                for ts in range(4):
                    for vf in range(2):
                        groups.append(v_group(ts, vf))
                return groups

            load_pass_weights(0)
            s_init()
            for g in make_proj_groups(0):
                g()
            for gseg in range(8):
                pp, sc = gseg // 4, gseg % 4
                scp = gseg % 2
                if gseg == 3:
                    load_pass_weights(1)
                if gseg == 4:
                    s_init()
                nxt = make_proj_groups(gseg + 1) if gseg < 7 else []
                per = (len(nxt) + 3) // 4 if nxt else 0
                for cl in range(4):
                    cc = sc * 4 + cl
                    gcc = gseg * 4 + cl
                    cp = gcc % 2
                    APS, APSK = PS[2 + cp], f"PS{2 + cp}"
                    for f in range(4):
                        P.op("tensor", lambda e, f=f, cl=cl, APS=APS, scp=scp: e.matmul(
                            APS[:, f * 128:(f + 1) * 128], gkneg[scp][cl][:, f * 128:(f + 1) * 128],
                            c_uneg[:, :], start=True, stop=True),
                            reads=[f"gkn{scp}:{cl}", "c_uneg"], writes=[APSK], rows=(0, 128), psum=APSK)
                    P.op("scalar", lambda e, cp=cp, APS=APS: e.activation(expA[cp][:, :], APS[:, 0:512],
                                                                          AF.Exp, bias=0.0, scale=1.0),
                         reads=[APSK], writes=[f"expA:{cp}"])
                    P.op("scalar", lambda e, cp=cp, APS=APS: e.activation(expNA[cp][:, :], APS[:, 0:512],
                                                                          AF.Exp, bias=0.0, scale=-1.0),
                         reads=[APSK], writes=[f"expNA:{cp}"])
                    for f in range(4):
                        P.op("vector", lambda e, f=f, cl=cl, cp=cp, scp=scp: e.tensor_tensor(
                            qe_sb[cp][f][:, :], qT_sb[scp][f][:, cl * 128:(cl + 1) * 128],
                            expA[cp][:, f * 128:(f + 1) * 128], ALU.mult),
                            reads=[f"qT{scp}:{f}", f"expA:{cp}"], writes=[f"qe{cp}:{f}"])
                        P.op("vector", lambda e, f=f, cl=cl, cp=cp, scp=scp: e.tensor_tensor(
                            ke_sb[cp][f][:, :], kT_sb[scp][f][:, cl * 128:(cl + 1) * 128],
                            expNA[cp][:, f * 128:(f + 1) * 128], ALU.mult),
                            reads=[f"kT{scp}:{f}", f"expNA:{cp}"], writes=[f"ke{cp}:{f}"])
                    take = nxt[cl * per:(cl + 1) * per] if nxt else []
                    n3 = (len(take) + 2) // 3
                    for g in take[:n3]:
                        g()
                    for f in range(4):
                        P.op("tensor", lambda e, f=f, cp=cp: e.transpose(PT[:, f * 128:(f + 1) * 128],
                                                                         ke_sb[cp][f][:, :], c_ident[:, :]),
                             reads=[f"ke{cp}:{f}", "c_ident"], writes=["PT"], rows=(0, 128), psum="PT")
                    P.op("vector", lambda e, cp=cp: e.tensor_copy(khn_sb[cp][:, :], PT[:, 0:512]),
                         reads=["PT"], writes=[f"khn:{cp}"])
                    for hb in range(2):
                        obank = [6, 4, 6, 4]
                        obk = ["PS6", "PS4", "PS6", "PS4"]
                        for i in range(4):
                            hl = hb * 4 + i
                            f = hl // 2
                            par = hl % 2
                            sps = 4 if par == 0 else 5
                            col = (i // 2) * 128
                            P.op("tensor", lambda e, f=f, par=par, sps=sps, col=col, cp=cp: e.matmul(
                                PS[sps][:, col:col + 128],
                                ke_sb[cp][f][par * 64:par * 64 + 64, :],
                                qe_sb[cp][f][par * 64:par * 64 + 64, :], start=True, stop=True),
                                reads=[f"ke{cp}:{f}", f"qe{cp}:{f}"], writes=[f"PS{sps}"],
                                rows=(par * 64, 64), psum=f"PS{sps}")
                        for j, spsn in enumerate((4, 5)):
                            P.op("vector", lambda e, j=j, spsn=spsn, cp=cp: e.tensor_tensor(
                                scm_sb[cp][j][:, :], PS[spsn][:, 0:256], c_mask4[:, :], ALU.mult),
                                reads=[f"PS{spsn}", "c_mask4"], writes=[f"scm{cp}:{j}"])
                        for i in range(4):
                            hl = hb * 4 + i
                            f = hl // 2
                            par = hl % 2
                            ob = obank[i]
                            obkey = obk[i]
                            scj = par
                            scol = (i // 2) * 128
                            P.op("tensor", lambda e, f=f, par=par, ob=ob, cp=cp: e.matmul(
                                PS[ob][:, 0:128], qe_sb[cp][f][par * 64:par * 64 + 64, :],
                                S_bf[f][par * 64:par * 64 + 64, :], start=True, stop=False),
                                reads=[f"qe{cp}:{f}", f"Sb:{f}"], writes=[obkey],
                                rows=(par * 64, 64), psum=obkey)
                            P.op("tensor", lambda e, hl=hl, ob=ob, scj=scj, scol=scol, cl=cl, cp=cp, scp=scp: e.matmul(
                                PS[ob][:, 0:128], scm_sb[cp][scj][:, scol:scol + 128],
                                v_sb[scp][cl][:, hl * 128:(hl + 1) * 128], start=False, stop=True),
                                reads=[f"scm{cp}:{scj}", f"v{scp}:{cl}"], writes=[obkey],
                                rows=(0, 128), psum=obkey)
                            P.op("vector", lambda e, i=i, ob=ob, hl=hl, cp=cp: e.tensor_scalar(
                                o_raw[cp][i][:, :], PS[ob][:, 0:128], 1.0, 0.0, ALU.mult, ALU.add,
                                accum_out=hsum[cp][:, hl:hl + 1]),
                                reads=[obkey], writes=[f"oraw{cp}:{i}", f"hsum:{cp}"], unsafe=True)
                            P.op("vector", lambda e, i=i, hl=hl, cp=cp: e.scalar_tensor_tensor(
                                sdump[:, :], o_raw[cp][i][:, :], 1.0, o_raw[cp][i][:, :], ALU.mult, ALU.mult,
                                accum_out=hsq[cp][:, hl:hl + 1]),
                                reads=[f"oraw{cp}:{i}"], writes=["sdump", f"hsq:{cp}"], unsafe=True)
                        h0 = hb * 4
                        P.op("vector", lambda e, h0=h0, cp=cp: e.tensor_tensor(
                            hwork[cp][:, 8:12], hsum[cp][:, h0:h0 + 4], hsum[cp][:, h0:h0 + 4], ALU.mult),
                            reads=[f"hsum:{cp}"], writes=[f"hwork:{cp}"], unsafe=True)
                        P.op("vector", lambda e, h0=h0, cp=cp: e.tensor_scalar(
                            hwork[cp][:, 4:8], hsq[cp][:, h0:h0 + 4], 1.0 / DV, EPS, ALU.mult, ALU.add),
                            reads=[f"hsq:{cp}"], writes=[f"hwork:{cp}"], unsafe=True)
                        P.op("vector", lambda e, cp=cp: e.scalar_tensor_tensor(
                            hwork[cp][:, 4:8], hwork[cp][:, 8:12], -1.0 / (DV * DV), hwork[cp][:, 4:8],
                            ALU.mult, ALU.add),
                            reads=[f"hwork:{cp}"], writes=[f"hwork:{cp}"], unsafe=True)
                        P.op("scalar", lambda e, cp=cp: e.activation(hwork[cp][:, 8:12], hwork[cp][:, 4:8],
                                                                     AF.Ln, bias=0.0, scale=1.0),
                             reads=[f"hwork:{cp}"], writes=[f"hwork:{cp}"])
                        P.op("scalar", lambda e, cp=cp: e.activation(hwork[cp][:, 12:16], hwork[cp][:, 8:12],
                                                                     AF.Exp, bias=0.0, scale=-0.5),
                             reads=[f"hwork:{cp}"], writes=[f"hwork:{cp}"])
                        P.op("vector", lambda e, h0=h0, cp=cp: e.scalar_tensor_tensor(
                            hwork[cp][:, 0:4], hsum[cp][:, h0:h0 + 4], -1.0 / DV, hwork[cp][:, 12:16],
                            ALU.mult, ALU.mult),
                            reads=[f"hsum:{cp}", f"hwork:{cp}"], writes=[f"hwork:{cp}"], unsafe=True)
                        for i in range(4):
                            hl = hb * 4 + i
                            op_ = cc % 2
                            P.op("scalar", lambda e, i=i, hl=hl, op_=op_, cp=cp: e.activation(
                                o_sb[op_][:, hl * 128:(hl + 1) * 128], o_raw[cp][i][:, :], AF.Identity,
                                bias=hwork[cp][:, 0 + i:1 + i], scale=hwork[cp][:, 12 + i:13 + i]),
                                reads=[f"oraw{cp}:{i}", f"hwork:{cp}"], writes=[f"osb:{op_}"])
                        if hb == 0:
                            for g in take[n3:2 * n3]:
                                g()
                    op_ = cc % 2
                    P.dma("gpsimd", lambda e, cc=cc, pp=pp, op_=op_: e.dma_start(
                        out=o_s[cc * 128:(cc + 1) * 128, pp * 1024:(pp + 1) * 1024], in_=o_sb[op_][:, :]),
                        reads=[f"osb:{op_}"], writes=[f"o_s:{cc}:{pp}"])
                    for f in range(4):
                        c0 = f * 128
                        P.op("tensor", lambda e, f=f, cl=cl, c0=c0, cp=cp, APS=APS, scp=scp: e.matmul(
                            APS[0:64, c0:c0 + 128], khn_sb[cp][:, f * 128:f * 128 + 64],
                            v_sb[scp][cl][:, (2 * f) * 128:(2 * f + 1) * 128], start=True, stop=True),
                            reads=[f"khn:{cp}", f"v{scp}:{cl}"], writes=[APSK], rows=(0, 128), psum=APSK)
                        P.op("tensor", lambda e, f=f, cl=cl, c0=c0, cp=cp, APS=APS, scp=scp: e.matmul(
                            APS[64:128, c0:c0 + 128], khn_sb[cp][:, f * 128 + 64:f * 128 + 128],
                            v_sb[scp][cl][:, (2 * f + 1) * 128:(2 * f + 2) * 128], start=True, stop=True,
                            tile_position=(0, 64)),
                            reads=[f"khn:{cp}", f"v{scp}:{cl}"], writes=[APSK], rows=(0, 128), psum=APSK)
                        P.op("vector", lambda e, f=f, c0=c0, cp=cp, APS=APS: e.tensor_tensor(
                            stmp[cp][:, :], S_f32[f][:, :], APS[:, c0:c0 + 128], ALU.add),
                            reads=[f"Sf:{f}", APSK], writes=[f"stmp{cp}"], unsafe=True)
                        P.op("vector", lambda e, f=f, cp=cp: e.tensor_scalar(
                            S_f32[f][:, :], stmp[cp][:, :], expA[cp][:, f * 128 + 127:f * 128 + 128], 0.0,
                            ALU.mult, ALU.add),
                            reads=[f"stmp{cp}", f"expA:{cp}"], writes=[f"Sf:{f}"], unsafe=True)
                        P.op("scalar", lambda e, f=f: e.copy(S_bf[f][:, :], S_f32[f][:, :]),
                             reads=[f"Sf:{f}"], writes=[f"Sb:{f}"])
                    for g in take[2 * n3:]:
                        g()
        P.barrier()

        with ExitStack() as sb1:
            sbb = lambda n, shp, d: sb1.enter_context(nc.sbuf_tensor(n, shp, d))
            wgb = [[sbb(f"wgb{p}_{k}", [128, 512], BF16) for k in range(NKT)] for p in range(2)]
            wub = [[sbb(f"wub{p}_{k}", [128, 512], BF16) for k in range(NKT)] for p in range(2)]
            bld_i = [sbb(f"bldi{j}", [128, 512], I32) for j in range(4)]
            gs_t = [sbb(f"gst{j}", [128, 512], BF16) for j in range(3)]
            wdl_g = [sbb(f"wdlg{j}", [128, 2048], I32) for j in range(2)]
            wdc_g = [sbb(f"wdcg{j}", [128, 2048], BF16) for j in range(2)]
            ub_t = [sbb(f"ubt{j}", [128, 512], BF16) for j in range(3)]
            hid_t = [sbb(f"hidt{j}", [128, 512], BF16) for j in range(3)]
            items = [("gu", si) for si in range(11)] + [("ga", si) for si in range(4)]
            pre_jobs = ([("wga", k) for k in range(NKT)] + [("wd", i) for i in range(NHS)]
                        + [("wo", k) for k in range(NKT)])

            def emit_pre(job):
                which, i = job
                j = i % 2
                q1 = "sync" if i % 2 == 0 else "scalar"
                q2 = "scalar" if i % 2 == 0 else "sync"
                if which == "wd":
                    P.dma(q1, lambda e, i=i, j=j: e.dma_start(
                        out=wdl_g[j][:, :], in_=wdownT_h[i * 128:(i + 1) * 128, :]),
                        writes=[f"wdlg{j}"])
                    P.op("vector", lambda e, i=i, j=j: e.tensor_scalar(
                        wdc_g[j][:, :], wdl_g[j][:, :], -128.0, 0.0, ALU.add, ALU.add),
                        reads=[f"wdlg{j}"], writes=[f"wdcg{j}"])
                    P.dma(q2, lambda e, i=i, j=j: e.dma_start(
                        out=wdbf_s[i * 128:(i + 1) * 128, :], in_=wdc_g[j][:, :]),
                        reads=[f"wdcg{j}"], writes=[f"wdbf:{i}"])
                else:
                    srch = woT_h if which == "wo" else wgaT_h
                    dsth = wobf_s if which == "wo" else wgabf_s
                    P.dma(q1, lambda e, i=i, j=j, srch=srch: e.dma_start(
                        out=wdl_g[j][:, 0:1024].bitcast(mybir.dt.float32), in_=srch[i * 128:(i + 1) * 128, 0:1024]),
                        writes=[f"wdlg{j}"])
                    P.dma(q1, lambda e, i=i, j=j, srch=srch: e.dma_start(
                        out=wdl_g[j][:, 1024:2048].bitcast(mybir.dt.float32), in_=srch[i * 128:(i + 1) * 128, 1024:2048]),
                        writes=[f"wdlg{j}"])
                    P.op("vector", lambda e, i=i, j=j: e.tensor_copy(
                        wdc_g[j][:, :], wdl_g[j][:, :].bitcast(mybir.dt.float32)),
                        reads=[f"wdlg{j}"], writes=[f"wdcg{j}"])
                    P.dma(q2, lambda e, i=i, j=j, dsth=dsth, which=which: e.dma_start(
                        out=dsth[i * 128:(i + 1) * 128, :], in_=wdc_g[j][:, :]),
                        reads=[f"wdcg{j}"], writes=[f"{which}bf2:{i}"])

            nci = 0
            for it_i, (kind, si) in enumerate(items):
                par = it_i % 2
                h0 = si * 512
                if kind == "gu":
                    for k in range(NKT):
                        j = (2 * k) % 4
                        j2 = (2 * k + 1) % 4
                        P.dma("sync", lambda e, k=k, h0=h0, j=j: e.dma_start(
                            out=bld_i[j][:, :], in_=wgateT_h[k * 128:(k + 1) * 128, h0:h0 + 512]),
                            writes=[f"bldi{j}"])
                        P.op("vector", lambda e, k=k, par=par, j=j: e.tensor_scalar(
                            wgb[par][k][:, :], bld_i[j][:, :], -128.0, 0.0, ALU.add, ALU.add),
                            reads=[f"bldi{j}"], writes=[f"wgb{par}:{k}"])
                        P.dma("sync", lambda e, k=k, h0=h0, j2=j2: e.dma_start(
                            out=bld_i[j2][:, :], in_=wupT_h[k * 128:(k + 1) * 128, h0:h0 + 512]),
                            writes=[f"bldi{j2}"])
                        P.op("vector", lambda e, k=k, par=par, j2=j2: e.tensor_scalar(
                            wub[par][k][:, :], bld_i[j2][:, :], -128.0, 0.0, ALU.add, ALU.add),
                            reads=[f"bldi{j2}"], writes=[f"wub{par}:{k}"])
                else:
                    for k in range(NKT):
                        q = "sync" if k % 2 == 0 else "scalar"
                        P.dma(q, lambda e, k=k, h0=h0, par=par: e.dma_start(
                            out=wgb[par][k][:, :], in_=wgabf_s[k * 128:(k + 1) * 128, h0:h0 + 512]),
                            reads=[f"wgabf2:{k}"], writes=[f"wgb{par}:{k}"])
                for tg in range(NTG):
                    slot = it_i * 4 + tg
                    lo = (slot * len(pre_jobs)) // 60
                    hi = ((slot + 1) * len(pre_jobs)) // 60
                    for job in pre_jobs[lo:hi]:
                        emit_pre(job)
                    for hs in range(4):
                        hh = 4 * si + hs
                        if kind == "gu":
                            psg, psgk = PS[(nci % 3)], f"PS{nci % 3}"
                            psu, psuk = PS[3 + (nci % 3)], f"PS{3 + nci % 3}"
                            nci += 1
                            for k in range(NKT):
                                P.op("tensor", lambda e, k=k, hs=hs, tg=tg, par=par, psg=psg: e.matmul(
                                    psg[:, 0:512], wgb[par][k][:, hs * 128:(hs + 1) * 128],
                                    xnT[k][:, tg * TG:(tg + 1) * TG], start=(k == 0), stop=(k == NKT - 1)),
                                    reads=[f"wgb{par}:{k}", f"xnT:{k}"], writes=[psgk], rows=(0, 128), psum=psgk)
                            for k in range(NKT):
                                P.op("tensor", lambda e, k=k, hs=hs, tg=tg, par=par, psu=psu: e.matmul(
                                    psu[:, 0:512], wub[par][k][:, hs * 128:(hs + 1) * 128],
                                    xnT[k][:, tg * TG:(tg + 1) * TG], start=(k == 0), stop=(k == NKT - 1)),
                                    reads=[f"wub{par}:{k}", f"xnT:{k}"], writes=[psuk], rows=(0, 128), psum=psuk)
                            jj = nci % 3
                            P.op("scalar", lambda e, hh=hh, jj=jj, psg=psg: e.activation(
                                gs_t[jj][:, :], psg[:, 0:512], AF.Silu, bias=0.0, scale=sb_sgate[:, hh:hh + 1]),
                                reads=[psgk, "sb_sgate"], writes=[f"gst{jj}"])
                            P.op("vector", lambda e, hh=hh, jj=jj, psu=psu: e.tensor_scalar(
                                ub_t[jj][:, :], psu[:, 0:512], sb_sup[:, hh:hh + 1], 0.0, ALU.mult, ALU.add),
                                reads=[psuk, "sb_sup"], writes=[f"ubt{jj}"])
                            P.op("vector", lambda e, jj=jj: e.tensor_tensor(
                                hid_t[jj][:, :], gs_t[jj][:, :], ub_t[jj][:, :], ALU.mult),
                                reads=[f"gst{jj}", f"ubt{jj}"], writes=[f"hidt{jj}"])
                            P.dma("gpsimd", lambda e, hh=hh, tg=tg, jj=jj: e.dma_start(
                                out=hid_s[hh * 128:(hh + 1) * 128, tg * TG:(tg + 1) * TG], in_=hid_t[jj][:, :]),
                                reads=[f"hidt{jj}"], writes=[f"hid_s:{hh}:{tg}"])
                        else:
                            ff = 4 * si + hs
                            psg, psgk = PS[(nci % 3)], f"PS{nci % 3}"
                            nci += 1
                            for k in range(NKT):
                                P.op("tensor", lambda e, k=k, hs=hs, tg=tg, par=par, psg=psg: e.matmul(
                                    psg[:, 0:512], wgb[par][k][:, hs * 128:(hs + 1) * 128],
                                    xnT[k][:, tg * TG:(tg + 1) * TG], start=(k == 0), stop=(k == NKT - 1)),
                                    reads=[f"wgb{par}:{k}", f"xnT:{k}"], writes=[psgk], rows=(0, 128), psum=psgk)
                            jj = nci % 3
                            P.op("scalar", lambda e, ff=ff, jj=jj, psg=psg: e.activation(
                                gs_t[jj][:, :], psg[:, 0:512], AF.Silu, bias=sb_bg[:, ff:ff + 1], scale=1.0),
                                reads=[psgk, "sb_bg"], writes=[f"gst{jj}"])
                            P.dma("gpsimd", lambda e, ff=ff, tg=tg, jj=jj: e.dma_start(
                                out=sg_s[ff * 128:(ff + 1) * 128, tg * TG:(tg + 1) * TG], in_=gs_t[jj][:, :]),
                                reads=[f"gst{jj}"], writes=[f"sg_s:{ff}:{tg}"])
        P.barrier()
        sx.close()

        with ExitStack() as sb2:
            sbb = lambda n, shp, d: sb2.enter_context(nc.sbuf_tensor(n, shp, d))
            wdb = [sbb(f"wdb{i}", [128, 1024], BF16) for i in range(NHS)]
            hidb = [[sbb(f"hb{p}_{i}", [128, 512], BF16) for i in range(NHS)] for p in range(2)]
            g_t = [sbb(f"gt{j}", [128, 512], BF16) for j in range(2)]
            gtmp = [sbb(f"gtmp{j}", [128, 512], F32) for j in range(2)]
            sdrep = sbb("sdrep", [128, D], F32)
            P.dma("sync", lambda e: e.dma_start(out=sdrep[:, :], in_=sdrep_h[:, :]), writes=["sdrep"])
            nci = 0
            for dh in range(2):
                for i in range(NHS):
                    q = "scalar" if i % 2 == 0 else "sync"
                    P.dma(q, lambda e, i=i, dh=dh: e.dma_start(
                        out=wdb[i][:, :], in_=wdbf_s[i * 128:(i + 1) * 128, dh * 1024:(dh + 1) * 1024]),
                        reads=[f"wdbf:{i}"], writes=[f"wdb:{i}"])
                for tg in range(NTG):
                    par = tg % 2
                    for i in range(NHS):
                        P.dma("sync", lambda e, i=i, tg=tg, par=par: e.dma_start(
                            out=hidb[par][i][:, :], in_=hid_s[i * 128:(i + 1) * 128, tg * TG:(tg + 1) * TG]),
                            reads=[f"hid_s:{i}:{tg}"], writes=[f"hb{par}:{i}"])
                    for ts in range(4):
                        trow = tg * 4 + ts
                        for dg in range(2):
                            dcol = dh * 2 + dg
                            ps, psk = PS[nci % 3], f"PS{nci % 3}"
                            nci += 1
                            for i in range(NHS):
                                P.op("tensor", lambda e, i=i, ts=ts, dg=dg, par=par, ps=ps: e.matmul(
                                    ps[:, 0:512], hidb[par][i][:, ts * 128:(ts + 1) * 128],
                                    wdb[i][:, dg * 512:(dg + 1) * 512],
                                    start=(i == 0), stop=(i == NHS - 1)),
                                    reads=[f"wdb:{i}", f"hb{par}:{i}"], writes=[psk], rows=(0, 128), psum=psk)
                            jj = nci % 2
                            P.op("vector", lambda e, jj=jj, ps=ps, dcol=dcol: e.tensor_tensor(
                                gtmp[jj][:, :], ps[:, 0:512], sdrep[:, dcol * 512:(dcol + 1) * 512], ALU.mult),
                                reads=[psk, "sdrep"], writes=[f"gtmp{jj}"])
                            P.op("scalar", lambda e, jj=jj: e.activation(
                                g_t[jj][:, :], gtmp[jj][:, :], AF.Silu, bias=0.0, scale=1.0),
                                reads=[f"gtmp{jj}"], writes=[f"gt{jj}"])
                            P.dma("gpsimd", lambda e, trow=trow, dcol=dcol, jj=jj: e.dma_start(
                                out=g_s[trow * 128:(trow + 1) * 128, dcol * 512:(dcol + 1) * 512],
                                in_=g_t[jj][:, :]),
                                reads=[f"gt{jj}"], writes=[f"g_s:{trow}:{dcol}"])
        P.barrier()

        with ExitStack() as sc_:
            sbb = lambda n, shp, d: sc_.enter_context(nc.sbuf_tensor(n, shp, d))
            wo_sb = [sbb(f"wo{k}", [128, 2048], BF16) for k in range(NKT)]
            sg_t = [[sbb(f"sgt{p}_{k}", [128, 512], BF16) for k in range(NKT)] for p in range(2)]
            oT_t = [[sbb(f"oTt{p}_{k}", [128, 512], BF16) for k in range(NKT)] for p in range(2)]
            go_t = [[sbb(f"got{p}_{k}", [128, 512], BF16) for k in range(NKT)] for p in range(2)]
            sfull = [sbb("sfull0", [128, D], BF16)] * 2
            fx = [sbb(f"fx{j}", [128, D], F32) for j in range(2)]
            fg = [sbb(f"fg{j}", [128, D], BF16) for j in range(2)]
            fo = [sbb(f"fo{j}", [128, D], F32) for j in range(2)]
            for k in range(NKT):
                q = "scalar" if k % 2 == 0 else "sync"
                P.dma(q, lambda e, k=k: e.dma_start(out=wo_sb[k][:, :], in_=wobf_s[k * 128:(k + 1) * 128, :]),
                      reads=[f"wobf2:{k}"], writes=[f"wo:{k}"])
            nci = 0
            for tg in range(NTG):
                for k in range(NKT):
                    P.dma("scalar", lambda e, k=k, tg=tg: e.dma_start(
                        out=sg_t[tg % 2][k][:, :], in_=sg_s[k * 128:(k + 1) * 128, tg * TG:(tg + 1) * TG]),
                        reads=[f"sg_s:{k}:{tg}"], writes=[f"sgt{tg % 2}:{k}"])
                    P.dma("sync", lambda e, k=k, tg=tg: e.dma_start(
                        out=oT_t[tg % 2][k][:, :], in_=o_s[tg * TG:(tg + 1) * TG, k * 128:(k + 1) * 128],
                        transpose=True),
                        reads=[f"o_s:{4 * tg + i}:{k // 8}" for i in range(4)], writes=[f"oTt{tg % 2}:{k}"])
                    P.op("vector", lambda e, k=k, tg=tg: e.tensor_tensor(
                        go_t[tg % 2][k][:, :], sg_t[tg % 2][k][:, :], oT_t[tg % 2][k][:, :], ALU.mult),
                        reads=[f"sgt{tg % 2}:{k}", f"oTt{tg % 2}:{k}"], writes=[f"got{tg % 2}:{k}"])
                for ts in range(4):
                    trow = tg * 4 + ts
                    sp = ts % 2
                    fp_ = trow % 2
                    P.dma("sync", lambda e, trow=trow, fp_=fp_: e.dma_start(
                        out=fx[fp_][:, :], in_=x_h[trow * 128:(trow + 1) * 128, :]),
                        writes=[f"fx{fp_}"])
                    P.dma("scalar", lambda e, trow=trow, fp_=fp_: e.dma_start(
                        out=fg[fp_][:, :], in_=g_s[trow * 128:(trow + 1) * 128, :]),
                        reads=[f"g_s:{trow}:{dc}" for dc in range(4)], writes=[f"fg{fp_}"])
                    for dg in range(4):
                        ps, psk = PS[2 + nci % 3], f"PS{2 + nci % 3}"
                        nci += 1
                        for k in range(NKT):
                            P.op("tensor", lambda e, k=k, ts=ts, dg=dg, ps=ps, tg=tg: e.matmul(
                                ps[:, 0:512], go_t[tg % 2][k][:, ts * 128:(ts + 1) * 128],
                                wo_sb[k][:, dg * 512:(dg + 1) * 512],
                                start=(k == 0), stop=(k == NKT - 1)),
                                reads=[f"wo:{k}", f"got{tg % 2}:{k}"], writes=[psk], rows=(0, 128), psum=psk)
                        P.op("scalar", lambda e, sp=sp, dg=dg, ps=ps: e.copy(
                            sfull[sp][:, dg * 512:(dg + 1) * 512], ps[:, 0:512]),
                            reads=[psk], writes=["sfull0"])
                    if debug:
                        P.dma("gpsimd", lambda e, trow=trow, sp=sp: e.dma_start(
                            out=s_s[trow * 128:(trow + 1) * 128, :], in_=sfull[sp][:, :]),
                            reads=["sfull0"], writes=[f"s_s:{trow}"])
                    P.op("vector", lambda e, sp=sp, fp_=fp_: e.scalar_tensor_tensor(
                        fo[fp_][:, :], sfull[sp][:, :], 1.0, fg[fp_][:, :], ALU.mult, ALU.mult),
                        reads=["sfull0", f"fg{fp_}"], writes=[f"fo{fp_}"])
                    P.op("vector", lambda e, fp_=fp_: e.tensor_tensor(fo[fp_][:, :], fx[fp_][:, :], fo[fp_][:, :],
                                                                      ALU.add),
                         reads=[f"fx{fp_}", f"fo{fp_}"], writes=[f"fo{fp_}"])
                    P.dma("sync", lambda e, trow=trow, fp_=fp_: e.dma_start(
                        out=out_h[trow * 128:(trow + 1) * 128, :], in_=fo[fp_][:, :]),
                        reads=[f"fo{fp_}"], writes=[f"out:{trow}"])
        P.barrier()

        P.resolve()
        with nc.Block() as block:
            P.emit(block)
    return nc


def _host_prep(inputs):
    f32 = np.float32
    Wq = np.asarray(inputs["L0_attn_Wq"], f32)
    Wk = np.asarray(inputs["L0_attn_Wk"], f32)
    Wkg1 = np.asarray(inputs["L0_attn_Wkg1"], f32)
    Wkg2 = np.asarray(inputs["L0_attn_Wkg2"], f32)
    bkg2 = np.asarray(inputs["L0_attn_bkg2"], f32)
    Wv = np.asarray(inputs["L0_attn_Wv"], f32)
    Wg = np.asarray(inputs["L0_attn_Wg"], f32)
    bg = np.asarray(inputs["L0_attn_bg"], f32)
    Wo = np.asarray(inputs["L0_attn_Wo"], f32)
    wq_g = np.asarray(inputs["L0_ffn_wq_gate"], np.int32)
    s_g = np.asarray(inputs["L0_ffn_s_gate"], f32)
    wq_u = np.asarray(inputs["L0_ffn_wq_up"], np.int32)
    s_u = np.asarray(inputs["L0_ffn_s_up"], f32)
    wq_d = np.asarray(inputs["L0_ffn_wq_down"], np.int32)
    s_d = np.asarray(inputs["L0_ffn_s_down"], f32)

    bf = ml_dtypes.bfloat16
    tri = np.triu(np.ones((128, 128), f32))
    shared = {
        "wqT": np.ascontiguousarray(Wq.T),
        "wkT": np.ascontiguousarray(Wk.T),
        "wvT": np.ascontiguousarray(Wv.T),
        "wgaT": np.ascontiguousarray(Wg.T),
        "woT": np.ascontiguousarray(Wo.T),
        "wkg1T": np.ascontiguousarray(Wkg1.T),
        "wkg2a": np.ascontiguousarray(np.concatenate([Wkg2.T, bkg2[None, :]], axis=0)),
        "wgateT": np.ascontiguousarray(wq_g.T),
        "wupT": np.ascontiguousarray(wq_u.T),
        "wdownT": np.ascontiguousarray(wq_d.T),
        "sgate_t": np.ascontiguousarray(s_g.reshape(NHS, 128).T),
        "sup_t": np.ascontiguousarray(s_u.reshape(NHS, 128).T),
        "sdown_t": np.ascontiguousarray(s_d.reshape(16, 128).T),
        "sdrep_in": np.ascontiguousarray(np.tile(s_d.reshape(1, D), (128, 1))),
        "bg_t": np.ascontiguousarray(bg.reshape(16, 128).T),
        "ident": np.eye(128, dtype=bf),
        "uneg": np.ascontiguousarray((tri * (-1.0 / 16.0)).astype(bf)),
        "mask4": np.ascontiguousarray(np.concatenate([tri, tri], axis=1).astype(bf)),
        "onescol": np.ones((128, 1), dtype=bf),
        "onesrow": np.ones((1, TG), dtype=f32),
        "neg128": np.full((1, 128), -128.0, dtype=f32),
    }
    return shared


def kernel(**inputs):
    x = np.asarray(inputs["L0_attn_x"], np.float32)
    shared = _host_prep(inputs)
    nc = build_nc()
    in_maps = [dict(x=np.ascontiguousarray(x[b]), **shared) for b in range(NCORE)]
    res = run_bass_kernel_spmd(nc, in_maps, core_ids=list(range(NCORE)))
    out = np.stack([np.asarray(res.results[b]["out"], np.float32) for b in range(NCORE)], axis=0)
    return out


# version 26
# speedup vs baseline: 1.0368x; 1.0368x over previous
import os
import numpy as np
import ml_dtypes
from contextlib import ExitStack

import concourse.bass as bass
import concourse.mybir as mybir
import concourse.alu_op_type as aot
from concourse.bass_utils import run_bass_kernel_spmd

F32 = mybir.dt.float32
BF16 = mybir.dt.bfloat16
I32 = mybir.dt.int32
AF = mybir.ActivationFunctionType
ALU = aot.AluOpType
AX = mybir.AxisListType.X

D = 2048
L = 2048
NCORE = 8
H = 16
DK = 64
DV = 128
HID = 5632
NKT = D // 128
TG = 512
NTG = L // TG
CH = 128
NCH = L // CH
SCALE = float(128.0 ** -0.5)
EPS = 1e-5
NHS = HID // 128
DEBUG = False

ENGS = ("sync", "gpsimd", "scalar", "vector", "tensor")
NSLOT = 16


class Op:
    __slots__ = ("eng", "fn", "reads", "writes", "dma", "rows", "psum",
                 "deps", "need_inc", "sem", "val", "waits", "barrier", "unsafe")

    def __init__(self, eng, fn, reads, writes, dma, rows, psum, barrier=False):
        self.unsafe = False
        self.eng = eng
        self.fn = fn
        self.reads = tuple(reads)
        self.writes = tuple(writes)
        self.dma = dma
        self.rows = rows
        self.psum = psum
        self.deps = []
        self.need_inc = dma
        self.sem = None
        self.val = 0
        self.waits = []
        self.barrier = barrier


class Prog:
    def __init__(self, nc, stack):
        self.nc = nc
        self.ops = []
        self.eng_sem = {e: stack.enter_context(nc.semaphore(f"S_{e}")) for e in ENGS}
        self.slot_sem = {}
        for q in ("sync", "gpsimd", "scalar"):
            for i in range(NSLOT):
                self.slot_sem[(q, i)] = stack.enter_context(nc.semaphore(f"D_{q}_{i}"))
        self.slot_rr = {"sync": 0, "gpsimd": 0, "scalar": 0}

    def op(self, eng, fn, reads=(), writes=(), rows=None, psum=None, unsafe=False):
        o = Op(eng, fn, reads, writes, False, rows, psum)
        o.unsafe = unsafe
        self.ops.append(o)

    def dma(self, eng, fn, reads=(), writes=()):
        o = Op(eng, fn, reads, writes, True, None, None)
        s = self.slot_rr[eng]
        self.slot_rr[eng] = (s + 1) % NSLOT
        o.writes = tuple(writes) + (f"__slot:{eng}:{s}",)
        o.sem = (eng, s)
        self.ops.append(o)

    def barrier(self):
        self.ops.append(Op(None, None, (), (), False, None, None, barrier=True))

    def resolve(self):
        region_w = {}
        region_r = {}
        pe_active = []
        last_pe = None
        for i, o in enumerate(self.ops):
            if o.barrier:
                region_r = {k: [] for k in region_r}
                pe_active = []
                continue
            deps = set()
            for r in o.reads:
                if r in region_w:
                    deps.add(region_w[r])
            for w in o.writes:
                if w in region_w:
                    deps.add(region_w[w])
                for rd in region_r.get(w, ()):
                    deps.add(rd)
            if o.eng == "tensor" and o.rows is not None:
                conflict = False
                for (rws, ps, oi) in pe_active:
                    if ps is not None and ps == o.psum:
                        a0, a1 = rws[0], rws[0] + rws[1]
                        b0, b1 = o.rows[0], o.rows[0] + o.rows[1]
                        if a1 <= b0 or b1 <= a0:
                            conflict = True
                if conflict and last_pe is not None:
                    deps.add(last_pe)
                    pe_active = []
                if o.rows == (0, 128):
                    pe_active = [(o.rows, o.psum, i)]
                else:
                    pe_active.append((o.rows, o.psum, i))
                last_pe = i
            kept = []
            for d in deps:
                do = self.ops[d]
                if ((not do.dma) and (not o.dma) and do.eng == o.eng
                        and o.eng == "tensor" and not do.unsafe):
                    continue
                kept.append(d)
            o.deps = kept
            for d in kept:
                self.ops[d].need_inc = True
            for w in o.writes:
                region_w[w] = i
                region_r[w] = []
            for r in o.reads:
                region_r.setdefault(r, []).append(i)
        eng_cnt = {e: 0 for e in ENGS}
        slot_cnt = {k: 0 for k in self.slot_sem}
        for o in self.ops:
            if o.barrier:
                continue
            if o.dma:
                slot_cnt[o.sem] += 16
                o.val = slot_cnt[o.sem]
            elif o.need_inc:
                eng_cnt[o.eng] += 1
                o.sem = o.eng
                o.val = eng_cnt[o.eng]
        maxseen = {e: {} for e in ENGS}
        self.barrier_waits = []
        eng_cnt2 = {e: 0 for e in ENGS}
        slot_cnt2 = {k: 0 for k in self.slot_sem}
        lastop = {e: None for e in ENGS}
        for o in self.ops:
            if o.barrier:
                for e in ENGS:
                    lo = lastop[e]
                    if lo is not None and (not lo.dma) and not lo.need_inc:
                        lo.need_inc = True
            elif not o.dma:
                lastop[o.eng] = o
        eng_cnt = {e: 0 for e in ENGS}
        slot_cnt = {k: 0 for k in self.slot_sem}
        for o in self.ops:
            if o.barrier:
                continue
            if o.dma:
                slot_cnt[o.sem] += 16
                o.val = slot_cnt[o.sem]
            elif o.need_inc:
                eng_cnt[o.eng] += 1
                o.sem = o.eng
                o.val = eng_cnt[o.eng]
            else:
                o.sem = None
        run_eng = {e: 0 for e in ENGS}
        run_slot = {k: 0 for k in self.slot_sem}
        for o in self.ops:
            if o.barrier:
                wl = [(("E", e), run_eng[e]) for e in ENGS if run_eng[e] > 0]
                wl += [(s, run_slot[s]) for s in self.slot_sem if run_slot[s] > 0]
                o.waits = wl
                for e in ENGS:
                    for (sk, v) in wl:
                        ms = maxseen[e]
                        if ms.get(sk, 0) < v:
                            ms[sk] = v
                continue
            wl = []
            for d in o.deps:
                do = self.ops[d]
                sk = ("E", do.sem) if not do.dma else do.sem
                if do.dma:
                    sk = do.sem
                else:
                    sk = ("E", do.eng)
                wl.append((sk, do.val))
            ms = maxseen[o.eng]
            wl2 = []
            for (sk, v) in sorted(set(wl)):
                if ms.get(sk, 0) >= v:
                    continue
                best = max((vv for (kk, vv) in wl if kk == sk), default=v)
                if ms.get(sk, 0) < best:
                    ms[sk] = best
                    wl2.append((sk, best))
            o.waits = wl2
            if o.dma:
                run_slot[o.sem] = o.val
            elif o.need_inc:
                run_eng[o.eng] = o.val

    def emit(self, block):
        def semof(sk):
            if isinstance(sk, tuple) and sk[0] == "E":
                return self.eng_sem[sk[1]]
            return self.slot_sem[sk]

        def mkbody(ename):
            def body(eng):
                for o in self.ops:
                    if o.barrier:
                        for (sk, v) in o.waits:
                            eng.wait_ge(semof(sk), v)
                        continue
                    if o.eng != ename:
                        continue
                    for (sk, v) in o.waits:
                        eng.wait_ge(semof(sk), v)
                    inst = o.fn(eng)
                    if o.dma:
                        inst.then_inc(self.slot_sem[o.sem], 16)
                    elif o.need_inc:
                        inst.then_inc(self.eng_sem[o.eng], 1)
            return body

        block.sync(mkbody("sync"))
        block.gpsimd(mkbody("gpsimd"))
        block.scalar(mkbody("scalar"))
        block.vector(mkbody("vector"))
        block.tensor(mkbody("tensor"))


def build_nc(debug=DEBUG):
    nc = bass.Bass("TRN2", target_bir_lowering=False)
    dt = nc.dram_tensor
    x_h = dt("x", [L, D], F32, kind="ExternalInput").ap()
    wqT_h = dt("wqT", [D, 1024], F32, kind="ExternalInput").ap()
    wkT_h = dt("wkT", [D, 1024], F32, kind="ExternalInput").ap()
    wvT_h = dt("wvT", [D, D], F32, kind="ExternalInput").ap()
    wgaT_h = dt("wgaT", [D, D], F32, kind="ExternalInput").ap()
    woT_h = dt("woT", [D, D], F32, kind="ExternalInput").ap()
    wkg1T_h = dt("wkg1T", [D, 16], F32, kind="ExternalInput").ap()
    wkg2a_h = dt("wkg2a", [17, 1024], F32, kind="ExternalInput").ap()
    wgateT_h = dt("wgateT", [D, HID], I32, kind="ExternalInput").ap()
    wupT_h = dt("wupT", [D, HID], I32, kind="ExternalInput").ap()
    wdownT_h = dt("wdownT", [HID, D], I32, kind="ExternalInput").ap()
    sgate_h = dt("sgate_t", [128, NHS], F32, kind="ExternalInput").ap()
    sup_h = dt("sup_t", [128, NHS], F32, kind="ExternalInput").ap()
    sdown_h = dt("sdown_t", [128, 16], F32, kind="ExternalInput").ap()
    bg_h = dt("bg_t", [128, 16], F32, kind="ExternalInput").ap()
    ident_h = dt("ident", [128, 128], BF16, kind="ExternalInput").ap()
    uneg_h = dt("uneg", [128, 128], BF16, kind="ExternalInput").ap()
    mask4_h = dt("mask4", [128, 256], BF16, kind="ExternalInput").ap()
    onescol_h = dt("onescol", [128, 1], BF16, kind="ExternalInput").ap()
    onesrow_h = dt("onesrow", [1, TG], F32, kind="ExternalInput").ap()
    neg128_h = dt("neg128", [1, 128], F32, kind="ExternalInput").ap()
    sdrep_h = dt("sdrep_in", [128, D], F32, kind="ExternalInput").ap()
    out_h = dt("out", [L, D], F32, kind="ExternalOutput").ap()
    skind = "ExternalOutput" if debug else "Internal"
    xn_s = dt("xn_s", [L, D], BF16, kind=skind).ap()
    wdbf_s = dt("wdbf_s", [HID, D], BF16, kind="Internal").ap()
    wobf_s = dt("wobf_s", [D, D], BF16, kind="Internal").ap()
    wqbf_s = dt("wqbf_s", [D, 1024], BF16, kind="Internal").ap()
    wkbf_s = dt("wkbf_s", [D, 1024], BF16, kind="Internal").ap()
    wvbf_s = dt("wvbf_s", [D, D], BF16, kind="Internal").ap()
    wkg1bf_s = dt("wkg1bf_s", [D, 16], BF16, kind="Internal").ap()
    wgabf_s = dt("wgabf_s", [D, D], BF16, kind="Internal").ap()
    o_s = dt("o_s", [L, D], BF16, kind=skind).ap()
    hid_s = dt("hid_s", [HID, L], BF16, kind=skind).ap()
    sg_s = dt("sg_s", [D, L], BF16, kind=skind).ap()
    g_s = dt("g_s", [L, D], BF16, kind=skind).ap()
    s_s = dt("s_s", [L, D], BF16, kind=skind).ap()

    st = ExitStack()
    with st:
        P = Prog(nc, st)
        sb = lambda name, shape, dtyp: st.enter_context(nc.sbuf_tensor(name, shape, dtyp))
        c_ident = sb("c_ident", [128, 128], BF16)
        c_uneg = sb("c_uneg", [128, 128], BF16)
        c_mask4 = sb("c_mask4", [128, 256], BF16)
        c_ones = sb("c_ones", [128, 1], BF16)
        c_neg128 = sb("c_neg128", [1, 128], F32)
        sb_sgate = sb("sb_sgate", [128, NHS], F32)
        sb_sup = sb("sb_sup", [128, NHS], F32)
        sb_sdown = sb("sb_sdown", [128, 16], F32)
        sb_bg = sb("sb_bg", [128, 16], F32)

        PS = [st.enter_context(nc.psum_tensor(f"PS{i}", [128, 512], F32)) for i in range(7)]
        PT = st.enter_context(nc.psum_tensor("PT", [128, 1024], BF16))

        for (t, h, key) in ((c_ident, ident_h, "c_ident"), (c_uneg, uneg_h, "c_uneg"),
                            (c_mask4, mask4_h, "c_mask4"), (c_ones, onescol_h, "c_ones"),
                            (c_neg128, neg128_h, "c_neg128"), (sb_sgate, sgate_h, "sb_sgate"),
                            (sb_sup, sup_h, "sb_sup"), (sb_sdown, sdown_h, "sb_sdown"),
                            (sb_bg, bg_h, "sb_bg")):
            P.dma("sync", lambda e, t=t, h=h: e.dma_start(out=t[:, :], in_=h[:, :]), writes=[key])


        for (src, dst, key, nch) in ((wkg1T_h, wkg1bf_s, "wkg1bf", 1), (wqT_h, wqbf_s, "wqbf", 2),
                                     (wkT_h, wkbf_s, "wkbf", 2), (wvT_h, wvbf_s, "wvbf", 4)):
            rows = src.shape[0] // nch
            for c in range(nch):
                P.dma("gpsimd", lambda e, src=src, dst=dst, c=c, rows=rows: e.dma_start(
                    out=dst[c * rows:(c + 1) * rows, :], in_=src[c * rows:(c + 1) * rows, :]),
                    writes=[f"{key}:{c}"])

        sx = ExitStack()
        xnT = [sx.enter_context(nc.sbuf_tensor(f"xnT{k}", [128, L], BF16)) for k in range(NKT)]
        with ExitStack() as s0:
            sb0 = lambda n, shp, d: s0.enter_context(nc.sbuf_tensor(n, shp, d))
            xb = [sb0(f"xb{i}", [128, D], F32) for i in range(8)]
            dump = sb0("dump", [128, D], BF16)
            dump2 = sb0("dump2", [128, D], BF16)
            xnb = [sb0(f"xnb{i}", [128, D], BF16) for i in range(2)]
            sb4 = [sb0(f"sb4_{i}", [128, 36], F32) for i in range(2)]
            for b4 in range(4):
                bp = b4 % 2
                stk = f"sb4_{bp}"
                for i in range(4):
                    tt = 4 * b4 + i
                    ii = tt % 8
                    P.dma("sync", lambda e, tt=tt, ii=ii: e.dma_start(out=xb[ii][:, :], in_=x_h[tt * 128:(tt + 1) * 128, :]),
                          writes=[f"xb{ii}"])
                for i in range(4):
                    ii = (4 * b4 + i) % 8
                    P.op("vector", lambda e, ii=ii, i=i, bp=bp: e.tensor_reduce(sb4[bp][:, i:i + 1], xb[ii][:, :], AX,
                                                                                ALU.add, negate=True),
                         reads=[f"xb{ii}"], writes=[stk], unsafe=True)
                    if i < 2:
                        P.op("scalar", lambda e, ii=ii, i=i, bp=bp: e.activation(dump2[:, :], xb[ii][:, :], AF.Square,
                                                                                 bias=0.0, scale=1.0,
                                                                                 accum_out=sb4[bp][:, 4 + i:5 + i]),
                             reads=[f"xb{ii}"], writes=["dump2", stk], unsafe=True)
                    else:
                        P.op("vector", lambda e, ii=ii, i=i, bp=bp: e.scalar_tensor_tensor(dump[:, :], xb[ii][:, :], 1.0,
                                                                                           xb[ii][:, :], ALU.mult, ALU.mult,
                                                                                           accum_out=sb4[bp][:, 4 + i:5 + i]),
                             reads=[f"xb{ii}"], writes=["dump", stk], unsafe=True)
                P.op("vector", lambda e, bp=bp: e.tensor_scalar(sb4[bp][:, 8:12], sb4[bp][:, 0:4], 1.0 / D, 0.0,
                                                                ALU.mult, ALU.add), reads=[stk], writes=[stk], unsafe=True)
                P.op("vector", lambda e, bp=bp: e.tensor_scalar(sb4[bp][:, 12:16], sb4[bp][:, 4:8], 1.0 / D, 0.0,
                                                                ALU.mult, ALU.add), reads=[stk], writes=[stk], unsafe=True)
                P.op("vector", lambda e, bp=bp: e.tensor_tensor(sb4[bp][:, 16:20], sb4[bp][:, 8:12], sb4[bp][:, 8:12],
                                                                ALU.mult), reads=[stk], writes=[stk], unsafe=True)
                P.op("vector", lambda e, bp=bp: e.scalar_tensor_tensor(sb4[bp][:, 20:24], sb4[bp][:, 12:16], EPS,
                                                                       sb4[bp][:, 16:20], ALU.add, ALU.subtract),
                     reads=[stk], writes=[stk], unsafe=True)
                P.op("scalar", lambda e, bp=bp: e.activation(sb4[bp][:, 24:28], sb4[bp][:, 20:24], AF.Ln,
                                                             bias=0.0, scale=1.0), reads=[stk], writes=[stk])
                P.op("scalar", lambda e, bp=bp: e.activation(sb4[bp][:, 28:32], sb4[bp][:, 24:28], AF.Exp,
                                                             bias=0.0, scale=-0.5), reads=[stk], writes=[stk])
                P.op("vector", lambda e, bp=bp: e.tensor_tensor(sb4[bp][:, 32:36], sb4[bp][:, 8:12], sb4[bp][:, 28:32],
                                                                ALU.mult), reads=[stk], writes=[stk], unsafe=True)
                for i in range(4):
                    tt = 4 * b4 + i
                    ii = tt % 8
                    xp = i % 2
                    P.op("scalar", lambda e, ii=ii, i=i, bp=bp, xp=xp: e.activation(
                        xnb[xp][:, :], xb[ii][:, :], AF.Identity,
                        bias=sb4[bp][:, 32 + i:33 + i], scale=sb4[bp][:, 28 + i:29 + i]),
                        reads=[f"xb{ii}", stk], writes=[f"xnb{xp}"])
                    if debug:
                        P.dma("gpsimd", lambda e, tt=tt, xp=xp: e.dma_start(
                            out=xn_s[tt * 128:(tt + 1) * 128, :], in_=xnb[xp][:, :]),
                            reads=[f"xnb{xp}"], writes=[f"xn_s:{tt}"])
                    for half in range(2):
                        for j in range(8):
                            kk = 8 * half + j
                            P.op("tensor", lambda e, kk=kk, j=j, xp=xp: e.transpose(
                                PT[:, j * 128:(j + 1) * 128], xnb[xp][:, kk * 128:(kk + 1) * 128], c_ident[:, :]),
                                reads=[f"xnb{xp}", "c_ident"], writes=["PT"], rows=(0, 128), psum="PT")
                        for j in range(8):
                            kk = 8 * half + j
                            P.op("vector", lambda e, kk=kk, j=j, tt=tt: e.tensor_copy(
                                xnT[kk][:, tt * 128:(tt + 1) * 128], PT[:, j * 128:(j + 1) * 128]),
                                reads=["PT"], writes=[f"xnT:{kk}"])
        P.barrier()

        with ExitStack() as sa:
            sba = lambda n, shp, d: sa.enter_context(nc.sbuf_tensor(n, shp, d))
            wq_sb = [sba(f"wq{k}", [128, 512], BF16) for k in range(NKT)]
            wk_sb = [sba(f"wk{k}", [128, 512], BF16) for k in range(NKT)]
            wv_sb = [sba(f"wv{k}", [128, 1024], BF16) for k in range(NKT)]
            wkg1_sb = [sba(f"wkg1{k}", [128, 16], BF16) for k in range(NKT)]
            wkg2_sb = sba("wkg2", [17, 512], F32)
            kg1_sb = sba("kg1", [17, 512], F32)
            qT_sb = [[sba(f"qT{c}_{f}", [128, 512], BF16) for f in range(4)] for c in range(2)]
            kT_sb = [[sba(f"kT{c}_{f}", [128, 512], BF16) for f in range(4)] for c in range(2)]
            gkneg = [[sba(f"gkn{c}_{t}", [128, 512], BF16) for t in range(4)] for c in range(2)]
            etmp = [sba(f"etmp{j}", [128, 512], BF16) for j in range(2)]
            v_sb = [[sba(f"v{c}_{t}", [128, 1024], BF16) for t in range(4)] for c in range(2)]
            expA = [sba(f"expA{c}", [128, 512], F32) for c in range(2)]
            expNA = [sba(f"expNA{c}", [128, 512], BF16) for c in range(2)]
            qe_sb = [[sba(f"qe{c}_{f}", [128, 128], BF16) for f in range(4)] for c in range(2)]
            ke_sb = [[sba(f"ke{c}_{f}", [128, 128], BF16) for f in range(4)] for c in range(2)]
            khn_sb = [sba(f"khn{c}", [128, 512], BF16) for c in range(2)]
            scm_sb = [[sba(f"scm{c}_{j}", [128, 256], BF16) for j in range(2)] for c in range(2)]
            S_f32 = [sba(f"Sf{f}", [128, 128], F32) for f in range(4)]
            S_bf = [sba(f"Sb{f}", [128, 128], BF16) for f in range(4)]
            o_raw = [[sba(f"oraw{c}_{j}", [128, 128], F32) for j in range(4)] for c in range(2)]
            hsum = [sba(f"hsum{c}", [128, 8], F32) for c in range(2)]
            hsq = [sba(f"hsq{c}", [128, 8], F32) for c in range(2)]
            hwork = [sba(f"hwork{c}", [128, 16], F32) for c in range(2)]
            sdump = sba("sdump", [128, 128], BF16)
            o_sb = [sba(f"osb{j}", [128, 1024], BF16) for j in range(2)]
            stmp = [sba(f"stmp{c}", [128, 128], F32) for c in range(2)]

            def load_pass_weights(pp):
                for k in range(NKT):
                    P.dma("scalar", lambda e, k=k, pp=pp: e.dma_start(
                        out=wkg1_sb[k][:, :], in_=wkg1bf_s[k * 128:(k + 1) * 128, 0:16]),
                        reads=["wkg1bf:0"], writes=[f"wkg1:{k}"])
                P.dma("sync", lambda e, pp=pp: e.dma_start(out=wkg2_sb[:, :], in_=wkg2a_h[:, pp * 512:pp * 512 + 512]),
                      writes=["wkg2"])
                if pp == 0:
                    P.dma("sync", lambda e: e.dma_start(out=kg1_sb[16:17, :], in_=onesrow_h[:, :]), writes=["kg1row"])
                for k in range(NKT):
                    P.dma("sync", lambda e, k=k, pp=pp: e.dma_start(
                        out=wq_sb[k][:, :], in_=wqbf_s[k * 128:(k + 1) * 128, pp * 512:pp * 512 + 512]),
                        reads=[f"wqbf:{k // 8}"], writes=[f"wq:{k}"])
                    P.dma("scalar", lambda e, k=k, pp=pp: e.dma_start(
                        out=wk_sb[k][:, :], in_=wkbf_s[k * 128:(k + 1) * 128, pp * 512:pp * 512 + 512]),
                        reads=[f"wkbf:{k // 8}"], writes=[f"wk:{k}"])
                for k in range(NKT):
                    q = "sync" if k % 2 == 0 else "scalar"
                    P.dma(q, lambda e, k=k, pp=pp: e.dma_start(
                        out=wv_sb[k][:, :], in_=wvbf_s[k * 128:(k + 1) * 128, pp * 1024:(pp + 1) * 1024]),
                        reads=[f"wvbf:{k // 4}"], writes=[f"wv:{k}"])

            def s_init():
                for f in range(4):
                    P.op("vector", lambda e, f=f: e.memset(S_f32[f][:, :], 0.0), writes=[f"Sf:{f}"])
                    P.op("vector", lambda e, f=f: e.memset(S_bf[f][:, :], 0.0), writes=[f"Sb:{f}"])

            def make_proj_groups(gseg):
                scp = gseg % 2
                T0 = (gseg % 4) * 512
                groups = []
                pa = [0]

                def q_group(f):
                    def emit():
                        ps, psk = PS[pa[0] % 2], f"PS{pa[0] % 2}"
                        pa[0] += 1
                        for k in range(NKT):
                            P.op("tensor", lambda e, k=k, ps=ps, f=f, T0=T0: e.matmul(
                                ps[:, 0:512], wq_sb[k][:, f * 128:(f + 1) * 128], xnT[k][:, T0:T0 + 512],
                                start=(k == 0), stop=(k == NKT - 1)),
                                reads=[f"wq:{k}", f"xnT:{k}"], writes=[psk], rows=(0, 128), psum=psk)
                        P.op("scalar", lambda e, ps=ps, f=f, scp=scp: e.copy(qT_sb[scp][f][:, :], ps[:, 0:512]),
                             reads=[psk], writes=[f"qT{scp}:{f}"])
                    return emit

                def k_group(f):
                    def emit():
                        ps, psk = PS[pa[0] % 2], f"PS{pa[0] % 2}"
                        pa[0] += 1
                        for k in range(NKT):
                            P.op("tensor", lambda e, k=k, ps=ps, f=f, T0=T0: e.matmul(
                                ps[:, 0:512], wk_sb[k][:, f * 128:(f + 1) * 128], xnT[k][:, T0:T0 + 512],
                                start=(k == 0), stop=(k == NKT - 1)),
                                reads=[f"wk:{k}", f"xnT:{k}"], writes=[psk], rows=(0, 128), psum=psk)
                        P.op("scalar", lambda e, ps=ps, f=f, scp=scp: e.activation(
                            kT_sb[scp][f][:, :], ps[:, 0:512], AF.Copy, bias=0.0, scale=SCALE),
                            reads=[psk], writes=[f"kT{scp}:{f}"])
                    return emit

                def kg1_group():
                    def emit():
                        ps, psk = PS[pa[0] % 2], f"PS{pa[0] % 2}"
                        pa[0] += 1
                        for k in range(NKT):
                            P.op("tensor", lambda e, k=k, ps=ps, T0=T0: e.matmul(
                                ps[0:16, 0:512], wkg1_sb[k][:, :], xnT[k][:, T0:T0 + 512],
                                start=(k == 0), stop=(k == NKT - 1)),
                                reads=[f"wkg1:{k}", f"xnT:{k}"], writes=[psk], rows=(0, 128), psum=psk)
                        P.op("vector", lambda e, ps=ps: e.tensor_copy(kg1_sb[0:16, :], ps[0:16, 0:512]),
                             reads=[psk], writes=["kg1"])
                    return emit

                def kgnat_group(ts):
                    def emit():
                        ps, psk = PS[pa[0] % 2], f"PS{pa[0] % 2}"
                        pa[0] += 1
                        P.op("tensor", lambda e, ts=ts, ps=ps: e.matmul(
                            ps[:, 0:512], kg1_sb[:, ts * 128:(ts + 1) * 128], wkg2_sb[:, :],
                            start=True, stop=True),
                            reads=["kg1", "kg1row", "wkg2"], writes=[psk], rows=(0, 17), psum=psk)
                        je = ts % 2
                        P.op("scalar", lambda e, ps=ps, je=je: e.activation(etmp[je][:, :], ps[:, 0:512],
                                                                            AF.Exp, bias=0.0, scale=-1.0),
                             reads=[psk], writes=[f"etmp{je}"])
                        P.op("scalar", lambda e, ts=ts, je=je, scp=scp: e.activation(
                            gkneg[scp][ts][:, :], etmp[je][:, :], AF.Ln, bias=1.0, scale=1.0),
                            reads=[f"etmp{je}"], writes=[f"gkn{scp}:{ts}"])
                    return emit

                def v_group(ts, vf):
                    def emit():
                        ps, psk = PS[pa[0] % 2], f"PS{pa[0] % 2}"
                        pa[0] += 1
                        for k in range(NKT):
                            P.op("tensor", lambda e, k=k, ts=ts, vf=vf, ps=ps, T0=T0: e.matmul(
                                ps[:, 0:512], xnT[k][:, T0 + ts * 128:T0 + ts * 128 + 128],
                                wv_sb[k][:, vf * 512:(vf + 1) * 512],
                                start=(k == 0), stop=(k == NKT - 1)),
                                reads=[f"wv:{k}", f"xnT:{k}"], writes=[psk], rows=(0, 128), psum=psk)
                        P.op("scalar", lambda e, ts=ts, vf=vf, ps=ps, scp=scp: e.copy(
                            v_sb[scp][ts][:, vf * 512:(vf + 1) * 512], ps[:, 0:512]),
                            reads=[psk], writes=[f"v{scp}:{ts}"])
                    return emit

                groups.append(kg1_group())
                for ts in range(4):
                    groups.append(kgnat_group(ts))
                for f in range(4):
                    groups.append(q_group(f))
                    groups.append(k_group(f))
                for ts in range(4):
                    for vf in range(2):
                        groups.append(v_group(ts, vf))
                return groups

            load_pass_weights(0)
            s_init()
            for g in make_proj_groups(0):
                g()
            for gseg in range(8):
                pp, sc = gseg // 4, gseg % 4
                scp = gseg % 2
                if gseg == 3:
                    load_pass_weights(1)
                if gseg == 4:
                    s_init()
                nxt = make_proj_groups(gseg + 1) if gseg < 7 else []
                per = (len(nxt) + 3) // 4 if nxt else 0
                for cl in range(4):
                    cc = sc * 4 + cl
                    gcc = gseg * 4 + cl
                    cp = gcc % 2
                    APS, APSK = PS[2 + cp], f"PS{2 + cp}"
                    for f in range(4):
                        P.op("tensor", lambda e, f=f, cl=cl, APS=APS, scp=scp: e.matmul(
                            APS[:, f * 128:(f + 1) * 128], gkneg[scp][cl][:, f * 128:(f + 1) * 128],
                            c_uneg[:, :], start=True, stop=True),
                            reads=[f"gkn{scp}:{cl}", "c_uneg"], writes=[APSK], rows=(0, 128), psum=APSK)
                    P.op("scalar", lambda e, cp=cp, APS=APS: e.activation(expA[cp][:, :], APS[:, 0:512],
                                                                          AF.Exp, bias=0.0, scale=1.0),
                         reads=[APSK], writes=[f"expA:{cp}"])
                    P.op("scalar", lambda e, cp=cp, APS=APS: e.activation(expNA[cp][:, :], APS[:, 0:512],
                                                                          AF.Exp, bias=0.0, scale=-1.0),
                         reads=[APSK], writes=[f"expNA:{cp}"])
                    for f in range(4):
                        P.op("vector", lambda e, f=f, cl=cl, cp=cp, scp=scp: e.tensor_tensor(
                            qe_sb[cp][f][:, :], qT_sb[scp][f][:, cl * 128:(cl + 1) * 128],
                            expA[cp][:, f * 128:(f + 1) * 128], ALU.mult),
                            reads=[f"qT{scp}:{f}", f"expA:{cp}"], writes=[f"qe{cp}:{f}"])
                        P.op("vector", lambda e, f=f, cl=cl, cp=cp, scp=scp: e.tensor_tensor(
                            ke_sb[cp][f][:, :], kT_sb[scp][f][:, cl * 128:(cl + 1) * 128],
                            expNA[cp][:, f * 128:(f + 1) * 128], ALU.mult),
                            reads=[f"kT{scp}:{f}", f"expNA:{cp}"], writes=[f"ke{cp}:{f}"])
                    take = nxt[cl * per:(cl + 1) * per] if nxt else []
                    n3 = (len(take) + 2) // 3
                    for g in take[:n3]:
                        g()
                    for f in range(4):
                        P.op("tensor", lambda e, f=f, cp=cp: e.transpose(PT[:, f * 128:(f + 1) * 128],
                                                                         ke_sb[cp][f][:, :], c_ident[:, :]),
                             reads=[f"ke{cp}:{f}", "c_ident"], writes=["PT"], rows=(0, 128), psum="PT")
                    P.op("vector", lambda e, cp=cp: e.tensor_copy(khn_sb[cp][:, :], PT[:, 0:512]),
                         reads=["PT"], writes=[f"khn:{cp}"])
                    for hb in range(2):
                        obank = [6, 4, 6, 4]
                        obk = ["PS6", "PS4", "PS6", "PS4"]
                        for i in range(4):
                            hl = hb * 4 + i
                            f = hl // 2
                            par = hl % 2
                            sps = 4 if par == 0 else 5
                            col = (i // 2) * 128
                            P.op("tensor", lambda e, f=f, par=par, sps=sps, col=col, cp=cp: e.matmul(
                                PS[sps][:, col:col + 128],
                                ke_sb[cp][f][par * 64:par * 64 + 64, :],
                                qe_sb[cp][f][par * 64:par * 64 + 64, :], start=True, stop=True),
                                reads=[f"ke{cp}:{f}", f"qe{cp}:{f}"], writes=[f"PS{sps}"],
                                rows=(par * 64, 64), psum=f"PS{sps}")
                        for j, spsn in enumerate((4, 5)):
                            P.op("vector", lambda e, j=j, spsn=spsn, cp=cp: e.tensor_tensor(
                                scm_sb[cp][j][:, :], PS[spsn][:, 0:256], c_mask4[:, :], ALU.mult),
                                reads=[f"PS{spsn}", "c_mask4"], writes=[f"scm{cp}:{j}"])
                        for i in range(4):
                            hl = hb * 4 + i
                            f = hl // 2
                            par = hl % 2
                            ob = obank[i]
                            obkey = obk[i]
                            scj = par
                            scol = (i // 2) * 128
                            P.op("tensor", lambda e, f=f, par=par, ob=ob, cp=cp: e.matmul(
                                PS[ob][:, 0:128], qe_sb[cp][f][par * 64:par * 64 + 64, :],
                                S_bf[f][par * 64:par * 64 + 64, :], start=True, stop=False),
                                reads=[f"qe{cp}:{f}", f"Sb:{f}"], writes=[obkey],
                                rows=(par * 64, 64), psum=obkey)
                            P.op("tensor", lambda e, hl=hl, ob=ob, scj=scj, scol=scol, cl=cl, cp=cp, scp=scp: e.matmul(
                                PS[ob][:, 0:128], scm_sb[cp][scj][:, scol:scol + 128],
                                v_sb[scp][cl][:, hl * 128:(hl + 1) * 128], start=False, stop=True),
                                reads=[f"scm{cp}:{scj}", f"v{scp}:{cl}"], writes=[obkey],
                                rows=(0, 128), psum=obkey)
                            P.op("vector", lambda e, i=i, ob=ob, hl=hl, cp=cp: e.tensor_scalar(
                                o_raw[cp][i][:, :], PS[ob][:, 0:128], 1.0, 0.0, ALU.mult, ALU.add,
                                accum_out=hsum[cp][:, hl:hl + 1]),
                                reads=[obkey], writes=[f"oraw{cp}:{i}", f"hsum:{cp}"], unsafe=True)
                            P.op("vector", lambda e, i=i, hl=hl, cp=cp: e.scalar_tensor_tensor(
                                sdump[:, :], o_raw[cp][i][:, :], 1.0, o_raw[cp][i][:, :], ALU.mult, ALU.mult,
                                accum_out=hsq[cp][:, hl:hl + 1]),
                                reads=[f"oraw{cp}:{i}"], writes=["sdump", f"hsq:{cp}"], unsafe=True)
                        h0 = hb * 4
                        P.op("vector", lambda e, h0=h0, cp=cp: e.tensor_tensor(
                            hwork[cp][:, 8:12], hsum[cp][:, h0:h0 + 4], hsum[cp][:, h0:h0 + 4], ALU.mult),
                            reads=[f"hsum:{cp}"], writes=[f"hwork:{cp}"], unsafe=True)
                        P.op("vector", lambda e, h0=h0, cp=cp: e.tensor_scalar(
                            hwork[cp][:, 4:8], hsq[cp][:, h0:h0 + 4], 1.0 / DV, EPS, ALU.mult, ALU.add),
                            reads=[f"hsq:{cp}"], writes=[f"hwork:{cp}"], unsafe=True)
                        P.op("vector", lambda e, cp=cp: e.scalar_tensor_tensor(
                            hwork[cp][:, 4:8], hwork[cp][:, 8:12], -1.0 / (DV * DV), hwork[cp][:, 4:8],
                            ALU.mult, ALU.add),
                            reads=[f"hwork:{cp}"], writes=[f"hwork:{cp}"], unsafe=True)
                        P.op("scalar", lambda e, cp=cp: e.activation(hwork[cp][:, 8:12], hwork[cp][:, 4:8],
                                                                     AF.Ln, bias=0.0, scale=1.0),
                             reads=[f"hwork:{cp}"], writes=[f"hwork:{cp}"])
                        P.op("scalar", lambda e, cp=cp: e.activation(hwork[cp][:, 12:16], hwork[cp][:, 8:12],
                                                                     AF.Exp, bias=0.0, scale=-0.5),
                             reads=[f"hwork:{cp}"], writes=[f"hwork:{cp}"])
                        P.op("vector", lambda e, h0=h0, cp=cp: e.scalar_tensor_tensor(
                            hwork[cp][:, 0:4], hsum[cp][:, h0:h0 + 4], -1.0 / DV, hwork[cp][:, 12:16],
                            ALU.mult, ALU.mult),
                            reads=[f"hsum:{cp}", f"hwork:{cp}"], writes=[f"hwork:{cp}"], unsafe=True)
                        for i in range(4):
                            hl = hb * 4 + i
                            op_ = cc % 2
                            P.op("scalar", lambda e, i=i, hl=hl, op_=op_, cp=cp: e.activation(
                                o_sb[op_][:, hl * 128:(hl + 1) * 128], o_raw[cp][i][:, :], AF.Identity,
                                bias=hwork[cp][:, 0 + i:1 + i], scale=hwork[cp][:, 12 + i:13 + i]),
                                reads=[f"oraw{cp}:{i}", f"hwork:{cp}"], writes=[f"osb:{op_}"])
                        if hb == 0:
                            for g in take[n3:2 * n3]:
                                g()
                    op_ = cc % 2
                    P.dma("gpsimd", lambda e, cc=cc, pp=pp, op_=op_: e.dma_start(
                        out=o_s[cc * 128:(cc + 1) * 128, pp * 1024:(pp + 1) * 1024], in_=o_sb[op_][:, :]),
                        reads=[f"osb:{op_}"], writes=[f"o_s:{cc}:{pp}"])
                    for f in range(4):
                        c0 = f * 128
                        P.op("tensor", lambda e, f=f, cl=cl, c0=c0, cp=cp, APS=APS, scp=scp: e.matmul(
                            APS[0:64, c0:c0 + 128], khn_sb[cp][:, f * 128:f * 128 + 64],
                            v_sb[scp][cl][:, (2 * f) * 128:(2 * f + 1) * 128], start=True, stop=True),
                            reads=[f"khn:{cp}", f"v{scp}:{cl}"], writes=[APSK], rows=(0, 128), psum=APSK)
                        P.op("tensor", lambda e, f=f, cl=cl, c0=c0, cp=cp, APS=APS, scp=scp: e.matmul(
                            APS[64:128, c0:c0 + 128], khn_sb[cp][:, f * 128 + 64:f * 128 + 128],
                            v_sb[scp][cl][:, (2 * f + 1) * 128:(2 * f + 2) * 128], start=True, stop=True,
                            tile_position=(0, 64)),
                            reads=[f"khn:{cp}", f"v{scp}:{cl}"], writes=[APSK], rows=(0, 128), psum=APSK)
                        P.op("vector", lambda e, f=f, c0=c0, cp=cp, APS=APS: e.tensor_tensor(
                            stmp[cp][:, :], S_f32[f][:, :], APS[:, c0:c0 + 128], ALU.add),
                            reads=[f"Sf:{f}", APSK], writes=[f"stmp{cp}"], unsafe=True)
                        P.op("vector", lambda e, f=f, cp=cp: e.tensor_scalar(
                            S_f32[f][:, :], stmp[cp][:, :], expA[cp][:, f * 128 + 127:f * 128 + 128], 0.0,
                            ALU.mult, ALU.add),
                            reads=[f"stmp{cp}", f"expA:{cp}"], writes=[f"Sf:{f}"], unsafe=True)
                        P.op("scalar", lambda e, f=f: e.copy(S_bf[f][:, :], S_f32[f][:, :]),
                             reads=[f"Sf:{f}"], writes=[f"Sb:{f}"])
                    for g in take[2 * n3:]:
                        g()
        P.barrier()

        with ExitStack() as sb1:
            sbb = lambda n, shp, d: sb1.enter_context(nc.sbuf_tensor(n, shp, d))
            wgb = [[sbb(f"wgb{p}_{k}", [128, 512], BF16) for k in range(NKT)] for p in range(2)]
            wub = [[sbb(f"wub{p}_{k}", [128, 512], BF16) for k in range(NKT)] for p in range(2)]
            bld_i = [sbb(f"bldi{j}", [128, 512], I32) for j in range(4)]
            gs_t = [sbb(f"gst{j}", [128, 512], BF16) for j in range(3)]
            wdl_g = [sbb(f"wdlg{j}", [128, 2048], I32) for j in range(2)]
            wdc_g = [sbb(f"wdcg{j}", [128, 2048], BF16) for j in range(2)]
            ub_t = [sbb(f"ubt{j}", [128, 512], BF16) for j in range(3)]
            hid_t = [sbb(f"hidt{j}", [128, 512], BF16) for j in range(3)]
            items = [("gu", si) for si in range(11)] + [("ga", si) for si in range(4)]
            pre_jobs = ([("wga", k) for k in range(NKT)] + [("wd", i) for i in range(NHS)]
                        + [("wo", k) for k in range(NKT)])

            def emit_pre(job):
                which, i = job
                j = i % 2
                q1 = "sync" if i % 2 == 0 else "scalar"
                q2 = "scalar" if i % 2 == 0 else "sync"
                if which == "wd":
                    P.dma(q1, lambda e, i=i, j=j: e.dma_start(
                        out=wdl_g[j][:, :], in_=wdownT_h[i * 128:(i + 1) * 128, :]),
                        writes=[f"wdlg{j}"])
                    P.op("vector", lambda e, i=i, j=j: e.tensor_scalar(
                        wdc_g[j][:, :], wdl_g[j][:, :], -128.0, 0.0, ALU.add, ALU.add),
                        reads=[f"wdlg{j}"], writes=[f"wdcg{j}"])
                    P.dma(q2, lambda e, i=i, j=j: e.dma_start(
                        out=wdbf_s[i * 128:(i + 1) * 128, :], in_=wdc_g[j][:, :]),
                        reads=[f"wdcg{j}"], writes=[f"wdbf:{i}"])
                else:
                    srch = woT_h if which == "wo" else wgaT_h
                    dsth = wobf_s if which == "wo" else wgabf_s
                    P.dma(q1, lambda e, i=i, j=j, srch=srch: e.dma_start(
                        out=wdl_g[j][:, 0:1024].bitcast(mybir.dt.float32), in_=srch[i * 128:(i + 1) * 128, 0:1024]),
                        writes=[f"wdlg{j}"])
                    P.dma(q1, lambda e, i=i, j=j, srch=srch: e.dma_start(
                        out=wdl_g[j][:, 1024:2048].bitcast(mybir.dt.float32), in_=srch[i * 128:(i + 1) * 128, 1024:2048]),
                        writes=[f"wdlg{j}"])
                    P.op("vector", lambda e, i=i, j=j: e.tensor_copy(
                        wdc_g[j][:, :], wdl_g[j][:, :].bitcast(mybir.dt.float32)),
                        reads=[f"wdlg{j}"], writes=[f"wdcg{j}"])
                    P.dma(q2, lambda e, i=i, j=j, dsth=dsth, which=which: e.dma_start(
                        out=dsth[i * 128:(i + 1) * 128, :], in_=wdc_g[j][:, :]),
                        reads=[f"wdcg{j}"], writes=[f"{which}bf2:{i}"])

            nci = 0
            for it_i, (kind, si) in enumerate(items):
                par = it_i % 2
                h0 = si * 512
                if kind == "gu":
                    for k in range(NKT):
                        j = (2 * k) % 4
                        j2 = (2 * k + 1) % 4
                        P.dma("sync", lambda e, k=k, h0=h0, j=j: e.dma_start(
                            out=bld_i[j][:, :], in_=wgateT_h[k * 128:(k + 1) * 128, h0:h0 + 512]),
                            writes=[f"bldi{j}"])
                        P.op("vector", lambda e, k=k, par=par, j=j: e.tensor_scalar(
                            wgb[par][k][:, :], bld_i[j][:, :], -128.0, 0.0, ALU.add, ALU.add),
                            reads=[f"bldi{j}"], writes=[f"wgb{par}:{k}"])
                        P.dma("sync", lambda e, k=k, h0=h0, j2=j2: e.dma_start(
                            out=bld_i[j2][:, :], in_=wupT_h[k * 128:(k + 1) * 128, h0:h0 + 512]),
                            writes=[f"bldi{j2}"])
                        P.op("vector", lambda e, k=k, par=par, j2=j2: e.tensor_scalar(
                            wub[par][k][:, :], bld_i[j2][:, :], -128.0, 0.0, ALU.add, ALU.add),
                            reads=[f"bldi{j2}"], writes=[f"wub{par}:{k}"])
                else:
                    for k in range(NKT):
                        q = "sync" if k % 2 == 0 else "scalar"
                        P.dma(q, lambda e, k=k, h0=h0, par=par: e.dma_start(
                            out=wgb[par][k][:, :], in_=wgabf_s[k * 128:(k + 1) * 128, h0:h0 + 512]),
                            reads=[f"wgabf2:{k}"], writes=[f"wgb{par}:{k}"])
                for tg in range(NTG):
                    slot = it_i * 4 + tg
                    lo = (slot * len(pre_jobs)) // 60
                    hi = ((slot + 1) * len(pre_jobs)) // 60
                    for job in pre_jobs[lo:hi]:
                        emit_pre(job)
                    for hs in range(4):
                        hh = 4 * si + hs
                        if kind == "gu":
                            psg, psgk = PS[(nci % 3)], f"PS{nci % 3}"
                            psu, psuk = PS[3 + (nci % 3)], f"PS{3 + nci % 3}"
                            nci += 1
                            for k in range(NKT):
                                P.op("tensor", lambda e, k=k, hs=hs, tg=tg, par=par, psg=psg: e.matmul(
                                    psg[:, 0:512], wgb[par][k][:, hs * 128:(hs + 1) * 128],
                                    xnT[k][:, tg * TG:(tg + 1) * TG], start=(k == 0), stop=(k == NKT - 1)),
                                    reads=[f"wgb{par}:{k}", f"xnT:{k}"], writes=[psgk], rows=(0, 128), psum=psgk)
                            for k in range(NKT):
                                P.op("tensor", lambda e, k=k, hs=hs, tg=tg, par=par, psu=psu: e.matmul(
                                    psu[:, 0:512], wub[par][k][:, hs * 128:(hs + 1) * 128],
                                    xnT[k][:, tg * TG:(tg + 1) * TG], start=(k == 0), stop=(k == NKT - 1)),
                                    reads=[f"wub{par}:{k}", f"xnT:{k}"], writes=[psuk], rows=(0, 128), psum=psuk)
                            jj = nci % 3
                            P.op("scalar", lambda e, hh=hh, jj=jj, psg=psg: e.activation(
                                gs_t[jj][:, :], psg[:, 0:512], AF.Silu, bias=0.0, scale=sb_sgate[:, hh:hh + 1]),
                                reads=[psgk, "sb_sgate"], writes=[f"gst{jj}"])
                            P.op("vector", lambda e, hh=hh, jj=jj, psu=psu: e.tensor_scalar(
                                ub_t[jj][:, :], psu[:, 0:512], sb_sup[:, hh:hh + 1], 0.0, ALU.mult, ALU.add),
                                reads=[psuk, "sb_sup"], writes=[f"ubt{jj}"])
                            P.op("vector", lambda e, jj=jj: e.tensor_tensor(
                                hid_t[jj][:, :], gs_t[jj][:, :], ub_t[jj][:, :], ALU.mult),
                                reads=[f"gst{jj}", f"ubt{jj}"], writes=[f"hidt{jj}"])
                            P.dma("gpsimd", lambda e, hh=hh, tg=tg, jj=jj: e.dma_start(
                                out=hid_s[hh * 128:(hh + 1) * 128, tg * TG:(tg + 1) * TG], in_=hid_t[jj][:, :]),
                                reads=[f"hidt{jj}"], writes=[f"hid_s:{hh}:{tg}"])
                        else:
                            ff = 4 * si + hs
                            psg, psgk = PS[(nci % 3)], f"PS{nci % 3}"
                            nci += 1
                            for k in range(NKT):
                                P.op("tensor", lambda e, k=k, hs=hs, tg=tg, par=par, psg=psg: e.matmul(
                                    psg[:, 0:512], wgb[par][k][:, hs * 128:(hs + 1) * 128],
                                    xnT[k][:, tg * TG:(tg + 1) * TG], start=(k == 0), stop=(k == NKT - 1)),
                                    reads=[f"wgb{par}:{k}", f"xnT:{k}"], writes=[psgk], rows=(0, 128), psum=psgk)
                            jj = nci % 3
                            P.op("scalar", lambda e, ff=ff, jj=jj, psg=psg: e.activation(
                                gs_t[jj][:, :], psg[:, 0:512], AF.Silu, bias=sb_bg[:, ff:ff + 1], scale=1.0),
                                reads=[psgk, "sb_bg"], writes=[f"gst{jj}"])
                            P.dma("gpsimd", lambda e, ff=ff, tg=tg, jj=jj: e.dma_start(
                                out=sg_s[ff * 128:(ff + 1) * 128, tg * TG:(tg + 1) * TG], in_=gs_t[jj][:, :]),
                                reads=[f"gst{jj}"], writes=[f"sg_s:{ff}:{tg}"])
        P.barrier()
        sx.close()

        with ExitStack() as sb2:
            sbb = lambda n, shp, d: sb2.enter_context(nc.sbuf_tensor(n, shp, d))
            wdb = [sbb(f"wdb{i}", [128, 1024], BF16) for i in range(NHS)]
            hidb = [[sbb(f"hb{p}_{i}", [128, 512], BF16) for i in range(NHS)] for p in range(2)]
            g_t = [sbb(f"gt{j}", [128, 512], BF16) for j in range(2)]
            gtmp = [sbb(f"gtmp{j}", [128, 512], F32) for j in range(2)]
            sdrep = sbb("sdrep", [128, D], F32)
            P.dma("sync", lambda e: e.dma_start(out=sdrep[:, :], in_=sdrep_h[:, :]), writes=["sdrep"])
            nci = 0
            for dh in range(2):
                for i in range(NHS):
                    q = "scalar" if i % 2 == 0 else "sync"
                    P.dma(q, lambda e, i=i, dh=dh: e.dma_start(
                        out=wdb[i][:, :], in_=wdbf_s[i * 128:(i + 1) * 128, dh * 1024:(dh + 1) * 1024]),
                        reads=[f"wdbf:{i}"], writes=[f"wdb:{i}"])
                for tg in range(NTG):
                    par = tg % 2
                    for i in range(NHS):
                        P.dma("sync", lambda e, i=i, tg=tg, par=par: e.dma_start(
                            out=hidb[par][i][:, :], in_=hid_s[i * 128:(i + 1) * 128, tg * TG:(tg + 1) * TG]),
                            reads=[f"hid_s:{i}:{tg}"], writes=[f"hb{par}:{i}"])
                    for ts in range(4):
                        trow = tg * 4 + ts
                        for dg in range(2):
                            dcol = dh * 2 + dg
                            ps, psk = PS[nci % 3], f"PS{nci % 3}"
                            nci += 1
                            for i in range(NHS):
                                P.op("tensor", lambda e, i=i, ts=ts, dg=dg, par=par, ps=ps: e.matmul(
                                    ps[:, 0:512], hidb[par][i][:, ts * 128:(ts + 1) * 128],
                                    wdb[i][:, dg * 512:(dg + 1) * 512],
                                    start=(i == 0), stop=(i == NHS - 1)),
                                    reads=[f"wdb:{i}", f"hb{par}:{i}"], writes=[psk], rows=(0, 128), psum=psk)
                            jj = nci % 2
                            P.op("vector", lambda e, jj=jj, ps=ps, dcol=dcol: e.tensor_tensor(
                                gtmp[jj][:, :], ps[:, 0:512], sdrep[:, dcol * 512:(dcol + 1) * 512], ALU.mult),
                                reads=[psk, "sdrep"], writes=[f"gtmp{jj}"])
                            P.op("scalar", lambda e, jj=jj: e.activation(
                                g_t[jj][:, :], gtmp[jj][:, :], AF.Silu, bias=0.0, scale=1.0),
                                reads=[f"gtmp{jj}"], writes=[f"gt{jj}"])
                            P.dma("gpsimd", lambda e, trow=trow, dcol=dcol, jj=jj: e.dma_start(
                                out=g_s[trow * 128:(trow + 1) * 128, dcol * 512:(dcol + 1) * 512],
                                in_=g_t[jj][:, :]),
                                reads=[f"gt{jj}"], writes=[f"g_s:{trow}:{dcol}"])
        P.barrier()

        with ExitStack() as sc_:
            sbb = lambda n, shp, d: sc_.enter_context(nc.sbuf_tensor(n, shp, d))
            wo_sb = [sbb(f"wo{k}", [128, 2048], BF16) for k in range(NKT)]
            sg_t = [[sbb(f"sgt{p}_{k}", [128, 512], BF16) for k in range(NKT)] for p in range(2)]
            oT_t = [[sbb(f"oTt{p}_{k}", [128, 512], BF16) for k in range(NKT)] for p in range(2)]
            go_t = [[sbb(f"got{p}_{k}", [128, 512], BF16) for k in range(NKT)] for p in range(2)]
            sfull = [sbb(f"sfull{j}", [128, D], BF16) for j in range(2)]
            fx = [sbb(f"fx{j}", [128, 1024], F32) for j in range(2)]
            fg = [sbb(f"fg{j}", [128, D], BF16) for j in range(2)]
            fo = [sbb(f"fo{j}", [128, D], F32) for j in range(2)]
            for k in range(NKT):
                q = "scalar" if k % 2 == 0 else "sync"
                P.dma(q, lambda e, k=k: e.dma_start(out=wo_sb[k][:, :], in_=wobf_s[k * 128:(k + 1) * 128, :]),
                      reads=[f"wobf2:{k}"], writes=[f"wo:{k}"])
            nci = 0
            for tg in range(NTG):
                for k in range(NKT):
                    P.dma("scalar", lambda e, k=k, tg=tg: e.dma_start(
                        out=sg_t[tg % 2][k][:, :], in_=sg_s[k * 128:(k + 1) * 128, tg * TG:(tg + 1) * TG]),
                        reads=[f"sg_s:{k}:{tg}"], writes=[f"sgt{tg % 2}:{k}"])
                    P.dma("sync", lambda e, k=k, tg=tg: e.dma_start(
                        out=oT_t[tg % 2][k][:, :], in_=o_s[tg * TG:(tg + 1) * TG, k * 128:(k + 1) * 128],
                        transpose=True),
                        reads=[f"o_s:{4 * tg + i}:{k // 8}" for i in range(4)], writes=[f"oTt{tg % 2}:{k}"])
                    P.op("vector", lambda e, k=k, tg=tg: e.tensor_tensor(
                        go_t[tg % 2][k][:, :], sg_t[tg % 2][k][:, :], oT_t[tg % 2][k][:, :], ALU.mult),
                        reads=[f"sgt{tg % 2}:{k}", f"oTt{tg % 2}:{k}"], writes=[f"got{tg % 2}:{k}"])
                for ts in range(4):
                    trow = tg * 4 + ts
                    sp = ts % 2
                    fp_ = trow % 2
                    for xh in range(2):
                        P.dma("sync", lambda e, trow=trow, xh=xh: e.dma_start(
                            out=fx[xh][:, :], in_=x_h[trow * 128:(trow + 1) * 128, xh * 1024:(xh + 1) * 1024]),
                            writes=[f"fx{xh}"])
                    P.dma("scalar", lambda e, trow=trow, fp_=fp_: e.dma_start(
                        out=fg[fp_][:, :], in_=g_s[trow * 128:(trow + 1) * 128, :]),
                        reads=[f"g_s:{trow}:{dc}" for dc in range(4)], writes=[f"fg{fp_}"])
                    for dg in range(4):
                        ps, psk = PS[2 + nci % 3], f"PS{2 + nci % 3}"
                        nci += 1
                        for k in range(NKT):
                            P.op("tensor", lambda e, k=k, ts=ts, dg=dg, ps=ps, tg=tg: e.matmul(
                                ps[:, 0:512], go_t[tg % 2][k][:, ts * 128:(ts + 1) * 128],
                                wo_sb[k][:, dg * 512:(dg + 1) * 512],
                                start=(k == 0), stop=(k == NKT - 1)),
                                reads=[f"wo:{k}", f"got{tg % 2}:{k}"], writes=[psk], rows=(0, 128), psum=psk)
                        P.op("scalar", lambda e, sp=sp, dg=dg, ps=ps: e.copy(
                            sfull[sp][:, dg * 512:(dg + 1) * 512], ps[:, 0:512]),
                            reads=[psk], writes=[f"sfull{sp}"])
                    if debug:
                        P.dma("gpsimd", lambda e, trow=trow, sp=sp: e.dma_start(
                            out=s_s[trow * 128:(trow + 1) * 128, :], in_=sfull[sp][:, :]),
                            reads=[f"sfull{sp}"], writes=[f"s_s:{trow}"])
                    P.op("vector", lambda e, sp=sp, fp_=fp_: e.scalar_tensor_tensor(
                        fo[fp_][:, :], sfull[sp][:, :], 1.0, fg[fp_][:, :], ALU.mult, ALU.mult),
                        reads=[f"sfull{sp}", f"fg{fp_}"], writes=[f"fo{fp_}"])
                    for xh in range(2):
                        P.op("vector", lambda e, fp_=fp_, xh=xh: e.tensor_tensor(
                            fo[fp_][:, xh * 1024:(xh + 1) * 1024], fx[xh][:, :],
                            fo[fp_][:, xh * 1024:(xh + 1) * 1024], ALU.add),
                            reads=[f"fx{xh}", f"fo{fp_}"], writes=[f"fo{fp_}"])
                    P.dma("sync", lambda e, trow=trow, fp_=fp_: e.dma_start(
                        out=out_h[trow * 128:(trow + 1) * 128, :], in_=fo[fp_][:, :]),
                        reads=[f"fo{fp_}"], writes=[f"out:{trow}"])
        P.barrier()

        P.resolve()
        with nc.Block() as block:
            P.emit(block)
    return nc


def _host_prep(inputs):
    f32 = np.float32
    Wq = np.asarray(inputs["L0_attn_Wq"], f32)
    Wk = np.asarray(inputs["L0_attn_Wk"], f32)
    Wkg1 = np.asarray(inputs["L0_attn_Wkg1"], f32)
    Wkg2 = np.asarray(inputs["L0_attn_Wkg2"], f32)
    bkg2 = np.asarray(inputs["L0_attn_bkg2"], f32)
    Wv = np.asarray(inputs["L0_attn_Wv"], f32)
    Wg = np.asarray(inputs["L0_attn_Wg"], f32)
    bg = np.asarray(inputs["L0_attn_bg"], f32)
    Wo = np.asarray(inputs["L0_attn_Wo"], f32)
    wq_g = np.asarray(inputs["L0_ffn_wq_gate"], np.int32)
    s_g = np.asarray(inputs["L0_ffn_s_gate"], f32)
    wq_u = np.asarray(inputs["L0_ffn_wq_up"], np.int32)
    s_u = np.asarray(inputs["L0_ffn_s_up"], f32)
    wq_d = np.asarray(inputs["L0_ffn_wq_down"], np.int32)
    s_d = np.asarray(inputs["L0_ffn_s_down"], f32)

    bf = ml_dtypes.bfloat16
    tri = np.triu(np.ones((128, 128), f32))
    shared = {
        "wqT": np.ascontiguousarray(Wq.T),
        "wkT": np.ascontiguousarray(Wk.T),
        "wvT": np.ascontiguousarray(Wv.T),
        "wgaT": np.ascontiguousarray(Wg.T),
        "woT": np.ascontiguousarray(Wo.T),
        "wkg1T": np.ascontiguousarray(Wkg1.T),
        "wkg2a": np.ascontiguousarray(np.concatenate([Wkg2.T, bkg2[None, :]], axis=0)),
        "wgateT": np.ascontiguousarray(wq_g.T),
        "wupT": np.ascontiguousarray(wq_u.T),
        "wdownT": np.ascontiguousarray(wq_d.T),
        "sgate_t": np.ascontiguousarray(s_g.reshape(NHS, 128).T),
        "sup_t": np.ascontiguousarray(s_u.reshape(NHS, 128).T),
        "sdown_t": np.ascontiguousarray(s_d.reshape(16, 128).T),
        "sdrep_in": np.ascontiguousarray(np.tile(s_d.reshape(1, D), (128, 1))),
        "bg_t": np.ascontiguousarray(bg.reshape(16, 128).T),
        "ident": np.eye(128, dtype=bf),
        "uneg": np.ascontiguousarray((tri * (-1.0 / 16.0)).astype(bf)),
        "mask4": np.ascontiguousarray(np.concatenate([tri, tri], axis=1).astype(bf)),
        "onescol": np.ones((128, 1), dtype=bf),
        "onesrow": np.ones((1, TG), dtype=f32),
        "neg128": np.full((1, 128), -128.0, dtype=f32),
    }
    return shared


def kernel(**inputs):
    x = np.asarray(inputs["L0_attn_x"], np.float32)
    shared = _host_prep(inputs)
    nc = build_nc()
    in_maps = [dict(x=np.ascontiguousarray(x[b]), **shared) for b in range(NCORE)]
    res = run_bass_kernel_spmd(nc, in_maps, core_ids=list(range(NCORE)))
    out = np.stack([np.asarray(res.results[b]["out"], np.float32) for b in range(NCORE)], axis=0)
    return out
